# version 09
# baseline (speedup 1.0000x reference)
import contextlib
import numpy as np
import ml_dtypes
import concourse.bass as bass
import concourse.mybir as mybir
from concourse.bass_utils import run_bass_kernel_spmd
from concourse.alu_op_type import AluOpType as aop

F32 = mybir.dt.float32
F16 = mybir.dt.float16
BF16 = mybir.dt.bfloat16
I32 = mybir.dt.int32
F8 = mybir.dt.float8e4
AF = mybir.ActivationFunctionType
AX = mybir.AxisListType
bfnp = ml_dtypes.bfloat16


class Cfg:
    def __init__(self, T=2048, H=4096, I=11008, O=4096, NIT=15, n_cores=8):
        self.T, self.H, self.I, self.O = T, H, I, O
        self.NIT = NIT
        self.n_cores = n_cores
        self.TT = T // 128
        self.KC = H // 128
        self.IC = I // 128
        self.BLK = H // 128
        self.K = I // 2
        self.W1 = 384
        ws = []
        off = 0
        while off < I:
            w = min(self.W1, I - off)
            ws.append((off, w))
            off += w
        self.panels1 = ws
        self.GK = 4
        self.NGR = (self.KC + self.GK - 1) // self.GK
        self.TG = 2
        assert self.TT % self.TG == 0
        assert self.TT <= 16
        self.W2 = 512 if O >= 512 else O
        assert O % self.W2 == 0
        self.OP = O // self.W2
        self.TH2 = (self.TT + 7) // 8
        self.BB = 2
        assert self.BLK % self.BB == 0
        self.G3 = 8
        gs = []
        ic = 0
        while ic < self.IC:
            g = min(self.G3, self.IC - ic)
            gs.append((ic, g))
            ic += g
        self.groups3 = gs
        self.HP = 1024 if O >= 1024 else O
        hp = []
        off = 0
        while off < O:
            w = min(self.HP, O - off)
            hp.append((off, w))
            off += w
        self.hpanels = hp
        self.TH0 = 8.0
        self.W0 = 4.0


class Sem:
    def __init__(self, st, nc, name):
        self.h = st.enter_context(nc.semaphore(name))
        self.v = 0


class Q:

    def __init__(self, name):
        self.name = name
        self.ops = []
        self._lastwait = {}

    def add(self, f):
        self.ops.append(f)

    def wait(self, sem, val=None):
        v = sem.v if val is None else val
        if v <= 0:
            return
        if self._lastwait.get(id(sem), -1) >= v:
            return
        self._lastwait[id(sem)] = v
        self.ops.append(lambda e, h=sem.h, v=v: e.wait_ge(h, v))

    @staticmethod
    def _ap(x):
        if isinstance(x, bass.AP) or not hasattr(x, "ap"):
            return x
        return x.ap()

    def dma(self, sem, out, in_, **kw):
        sem.v += 16
        out, in_ = self._ap(out), self._ap(in_)
        def f(e, out=out, in_=in_, kw=kw, h=sem.h):
            e.dma_start(out=out, in_=in_, **kw).then_inc(h, 16)
        self.ops.append(f)

    def run(self, fname, *args, sem=None, inc=1, **kw):
        if sem is not None:
            sem.v += inc
        args = tuple(self._ap(a) for a in args)
        kw = {k: (self._ap(v) if k in ("out", "in_", "in0", "in1") else v)
              for k, v in kw.items()}
        def f(e, fname=fname, args=args, kw=kw, sem=sem, inc=inc):
            r = getattr(e, fname)(*args, **kw)
            if sem is not None:
                r.then_inc(sem.h, inc)
        self.ops.append(f)


def flush_block(nc, name, qs):
    with nc.Block(name) as blk:
        for q in qs.values():
            if q.ops:
                runner = getattr(blk, q.name)
                def body(eng, ops=q.ops):
                    for f in ops:
                        f(eng)
                runner(body)


def new_queues():
    return {n: Q(n) for n in ("sync", "scalar", "vector", "gpsimd", "tensor")}


def build_nc(cfg: Cfg):
    T, H, I, O = cfg.T, cfg.H, cfg.I, cfg.O
    TT, KC, IC, BLK = cfg.TT, cfg.KC, cfg.IC, cfg.BLK
    nc = bass.Bass("TRN2", target_bir_lowering=False)
    st = contextlib.ExitStack()

    x_d = nc.dram_tensor("x", [T, H], F32, kind="ExternalInput").ap()
    ng = 4 * 128
    wg_sz = H * I
    wg_d = nc.dram_tensor("wg", [wg_sz], F32, kind="ExternalInput").ap()
    wu_d = nc.dram_tensor("wu", [wg_sz], F32, kind="ExternalInput").ap()
    wdt_d = nc.dram_tensor("wdt", [IC, 128, O], F32, kind="ExternalInput").ap()
    pkt_d = nc.dram_tensor("pkt", [BLK, 64, O], I32, kind="ExternalInput").ap()
    scb_d = nc.dram_tensor("scb", [BLK, 128, O], F32, kind="ExternalInput").ap()
    out_d = nc.dram_tensor("out", [T, O], F32, kind="ExternalOutput").ap()
    xn_d = nc.dram_tensor("xn_s", [T, H], BF16, kind="Internal").ap()
    s_d = nc.dram_tensor("s_s", [TT, 128, I], F16, kind="Internal").ap()
    sup_d = nc.dram_tensor("sup_s", [TT, 128, I], BF16, kind="Internal").ap()
    mask_d = nc.dram_tensor("mask_s", [TT, 128, I], F8, kind="Internal").ap()
    hm_d = nc.dram_tensor("hm_s", [I, T], BF16, kind="Internal").ap()
    g_d = nc.dram_tensor("g_s", [T, O], F32, kind="Internal").ap()
    wq_dram = nc.dram_tensor("wq_s", [BLK, 128, O], BF16, kind="Internal").ap()

    gran_off = {}
    off = 0
    for p, (poff, W) in enumerate(cfg.panels1):
        for g in range(cfg.NGR):
            ck = min(cfg.GK, KC - g * cfg.GK)
            gran_off[(p, g)] = (off, ck, W)
            off += ck * 128 * W
    assert off == wg_sz

    sems = {}
    for name in ("xld", "v0", "a0", "spxn", "trx",
                 "pegr", "mm1", "e1a", "e1v", "sp1",
                 "pkld", "scld", "dq", "evg", "gsp", "peb",
                 "sld", "chA", "chD", "mk", "msp",
                 "hmld", "hmtt", "hmsp", "trh", "wdld", "xgld", "fl", "outp",
                 "wqsp"):
        sems[name] = Sem(st, nc, name)
    S = type("S", (), sems)
    wld = [Sem(st, nc, f"wld{g}") for g in range(cfg.NGR)]
    xld2 = [Sem(st, nc, f"xld{i}") for i in range(2)]
    pkld2 = [Sem(st, nc, f"pkld{i}") for i in range(2)]
    scld2 = [Sem(st, nc, f"scld{i}") for i in range(2)]
    sld2 = {c: Sem(st, nc, f"sld{c}") for c in "AD"}
    msp2 = {c: Sem(st, nc, f"msp{c}") for c in "AD"}
    hmld2 = [Sem(st, nc, f"hmld{i}") for i in range(2)]
    trh2 = [Sem(st, nc, f"trh{i}") for i in range(2)]
    wdld2 = [Sem(st, nc, f"wdld{i}") for i in range(2)]
    xgld2 = [Sem(st, nc, f"xgld{i}") for i in range(2)]
    wqld2 = [Sem(st, nc, f"wqld{i}") for i in range(2)]
    gsp2 = [Sem(st, nc, f"gsp{i}") for i in range(2)]
    asp2 = {c: Sem(st, nc, f"asp{c}") for c in "AD"}

    stats = st.enter_context(nc.sbuf_tensor("stats", [128, 16], F32))
    th_a = st.enter_context(nc.sbuf_tensor("th_a", [128, TT], F32))
    th_b = st.enter_context(nc.sbuf_tensor("th_b", [128, TT], F32))
    cnt_t = st.enter_context(nc.sbuf_tensor("cnt_t", [128, TT], F32))
    sgn_t = st.enter_context(nc.sbuf_tensor("sgn_t", [128, TT], F32))
    xnT_ctx = nc.sbuf_tensor("xnT", [128, KC * T], BF16)
    xnT = xnT_ctx.__enter__()

    qs = new_queues()
    with (nc.sbuf_tensor("xbuf", [128, 2 * H], F32) as xbuf,
          nc.sbuf_tensor("xnst", [128, 2 * H], BF16) as xnst,
          nc.sbuf_tensor("junk0", [128, H], F8) as junk8):
        SY, SC, VE = qs["sync"], qs["scalar"], qs["vector"]
        rel_x = {}
        for tt in range(TT):
            par = tt % 2
            xb = xbuf[:, par * H:(par + 1) * H]
            xs = xnst[:, par * H:(par + 1) * H]
            if tt - 2 >= 0:
                for sem, val in rel_x[tt - 2]:
                    SY.wait(sem, val)
            SY.dma(xld2[par], xb, x_d[tt * 128:(tt + 1) * 128, :])
            c = (tt % 2) * 8
            VE.wait(xld2[par])
            VE.run("reduce_sum", stats[:, c:c + 1], xb, axis=AX.X, sem=S.v0)
            VE.wait(S.v0)
            VE.run("tensor_scalar", out=stats[:, c + 1:c + 2], in0=stats[:, c:c + 1],
                   scalar1=-1.0 / H, scalar2=None, op0=aop.mult, sem=S.v0)
            SC.wait(S.v0)
            SC.wait(xld2[par], xld2[par].v)
            SC.run("activation", out=junk8[:, :H], in_=xb, func=AF.Square,
                   bias=stats[:, c + 1:c + 2], scale=1.0,
                   accum_out=stats[:, c + 2:c + 3], sem=S.a0)
            VE.wait(S.a0)
            VE.run("tensor_scalar", out=stats[:, c + 3:c + 4], in0=stats[:, c + 2:c + 3],
                   scalar1=1.0 / H, scalar2=1e-5, op0=aop.mult, op1=aop.add, sem=S.v0)
            SC.wait(S.v0)
            SC.run("activation", out=stats[:, c + 4:c + 5], in_=stats[:, c + 3:c + 4],
                   func=AF.Sqrt, sem=S.a0)
            VE.wait(S.a0)
            VE.run("reciprocal", stats[:, c + 5:c + 6], stats[:, c + 4:c + 5], sem=S.v0)
            VE.wait(S.v0)
            if tt - 2 >= 0:
                VE.wait(S.spxn)
            xs_perm = xs.rearrange("p (b r c) -> p b c r", b=KC, r=2)
            VE.run("tensor_scalar", out=xs_perm, in0=xb,
                   scalar1=stats[:, c + 1:c + 2], scalar2=stats[:, c + 5:c + 6],
                   op0=aop.add, op1=aop.mult, sem=S.v0)
            rel_x[tt] = [(S.a0, S.a0.v), (S.v0, S.v0.v)]
            SY.wait(S.v0)
            SY.dma(S.spxn, xn_d[tt * 128:(tt + 1) * 128, :], xs)
        SCQ = qs["scalar"]
        SCQ.wait(S.spxn)
        for k in range(KC):
            SCQ.dma(S.trx, xnT[:, k * T:(k + 1) * T],
                    xn_d[:, k * 128:(k + 1) * 128], transpose=True)
    flush_block(nc, "p0", qs)

    qs = new_queues()
    slotW = cfg.GK * cfg.W1
    with (nc.sbuf_tensor("wgr", [128, cfg.NGR * slotW], BF16) as wgr,
          nc.sbuf_tensor("wur", [128, cfg.NGR * slotW], BF16) as wur,
          nc.sbuf_tensor("sst", [128, 2 * cfg.TG * cfg.W1], F16) as sst,
          nc.sbuf_tensor("supst", [128, 2 * cfg.TG * cfg.W1], BF16) as supst,
          nc.sbuf_tensor("pkq", [128, 2 * 1024], I32) as pkq,
          nc.sbuf_tensor("nibq", [128, 1024], I32) as nibq,
          nc.sbuf_tensor("scq", [128, 2 * 1024], F32) as scq,
          nc.sbuf_tensor("wqst", [128, 2 * 1024], BF16) as wqst,
          contextlib.ExitStack() as pst):
        psum = [pst.enter_context(nc.psum_tensor(f"p1_{i}", [128, 512], F32))
                for i in range(8)]
        SY, SC, VE, GP, TE = (qs[n] for n in ("sync", "scalar", "vector", "gpsimd", "tensor"))
        TE.wait(S.trx)
        NGROUP = TT // cfg.TG
        npan = len(cfg.panels1)
        ev_hist = {}
        sp_hist = {}
        gq_hist = {}
        def load_gran(p, g):
            offs, ck, W = gran_off[(p, g)]
            if p > 0:
                if g < cfg.NGR - 1:
                    GP.wait(S.pegr, (p - 1) * (cfg.NGR - 1) + g + 1)
                else:
                    GP.wait(S.mm1, p * NGROUP)
            for (ring, src) in ((wgr, wg_d), (wur, wu_d)):
                dst = ring[:, g * slotW + 0: g * slotW + ck * W]
                dst = dst.rearrange("p (c w) -> p c w", c=ck)
                srcap = src[offs: offs + ck * 128 * W]
                srcap = srcap.rearrange("(c p w) -> p c w", c=ck, p=128)
                GP.dma(wld[g], dst, srcap)
        for g in range(cfg.NGR):
            load_gran(0, g)
        DCW = min(512, O)
        batches = [(b0, c0) for b0 in range(0, BLK, 2) for c0 in range(0, O, DCW)]
        NBAT = len(batches)
        dqb_hist = {}
        deq_i = [0]

        def deq_step():
            bi = deq_i[0]
            if bi >= NBAT:
                return
            deq_i[0] += 1
            b0, c0 = batches[bi]
            par = bi % 2
            bw2 = 2 * DCW
            if bi - 2 >= 0:
                SY.wait(S.dq, dqb_hist[bi - 2])
            src = pkt_d[b0:b0 + 2, :, c0:c0 + DCW].rearrange("b p w -> p b w")
            SY.dma(pkld2[par], pkq[0:64, par * bw2:(par + 1) * bw2].rearrange(
                "p (b w) -> p b w", b=2), src)
            SY.dma(pkld2[par], pkq[64:128, par * bw2:(par + 1) * bw2].rearrange(
                "p (b w) -> p b w", b=2), src)
            scsrc = scb_d[b0:b0 + 2, :, c0:c0 + DCW].rearrange("b p w -> p b w")
            SY.dma(scld2[par], scq[:, par * bw2:(par + 1) * bw2].rearrange(
                "p (b w) -> p b w", b=2), scsrc)
            VE.wait(pkld2[par])
            VE.wait(S.dq)
            VE.wait(S.wqsp)
            VE.run("tensor_scalar", out=nibq[0:64, :bw2],
                   in0=pkq[0:64, par * bw2:(par + 1) * bw2], scalar1=4,
                   scalar2=None, op0=aop.logical_shift_right, sem=S.dq)
            VE.run("tensor_scalar", out=nibq[64:128, :bw2],
                   in0=pkq[64:128, par * bw2:(par + 1) * bw2], scalar1=15,
                   scalar2=None, op0=aop.bitwise_and, sem=S.dq)
            VE.wait(scld2[par])
            VE.wait(S.dq)
            VE.run("scalar_tensor_tensor", out=wqst[:, par * bw2:(par + 1) * bw2],
                   in0=nibq[:, :bw2], scalar=8.0,
                   in1=scq[:, par * bw2:(par + 1) * bw2], op0=aop.subtract,
                   op1=aop.mult, sem=S.dq)
            dqb_hist[bi] = S.dq.v
            GP.wait(S.dq, dqb_hist[bi])
            GP.dma(S.wqsp, wq_dram[b0, :, c0:c0 + DCW],
                   wqst[:, par * bw2: par * bw2 + DCW])
            GP.dma(S.wqsp, wq_dram[b0 + 1, :, c0:c0 + DCW],
                   wqst[:, par * bw2 + DCW:(par + 1) * bw2])

        gidx = 0
        ngroups_total = len(cfg.panels1) * NGROUP
        for p, (poff, W) in enumerate(cfg.panels1):
            for tg in range(NGROUP):
                par = tg % 2
                tiles = (cfg.TG * tg, cfg.TG * tg + 1)
                if gidx - 2 >= 0:
                    ea, ev = ev_hist[gidx - 2]
                    TE.wait(S.e1a, ea)
                    TE.wait(S.e1v, ev)
                for k in range(KC):
                    g = k // cfg.GK
                    if tg == 0:
                        TE.wait(wld[g], 32 * (p + 1))
                    ck_local = k % cfg.GK
                    woff = g * slotW + ck_local * W
                    for tl, ti in enumerate(tiles):
                        lhs = xnT[:, k * T + ti * 128: k * T + (ti + 1) * 128]
                        pg = psum[par * 4 + 2 * tl][:, 0:W]
                        pu = psum[par * 4 + 2 * tl + 1][:, 0:W]
                        kw = dict(start=(k == 0), stop=(k == KC - 1))
                        last_of_group = (k == KC - 1 and tl == len(tiles) - 1)
                        rel_gran = (tg == NGROUP - 1 and ck_local == 0 and g > 0
                                    and tl == 0)
                        if rel_gran:
                            TE.run("matmul", pg, lhs, wgr[:, woff:woff + W],
                                   sem=S.pegr, **kw)
                            if p + 1 < npan:
                                load_gran(p + 1, g - 1)
                        else:
                            TE.run("matmul", pg, lhs, wgr[:, woff:woff + W], **kw)
                        if last_of_group:
                            TE.run("matmul", pu, lhs, wur[:, woff:woff + W],
                                   sem=S.mm1, **kw)
                        else:
                            TE.run("matmul", pu, lhs, wur[:, woff:woff + W], **kw)
                gq_hist[gidx] = S.mm1.v
                if tg == NGROUP - 1 and p + 1 < npan:
                    load_gran(p + 1, cfg.NGR - 1)
                SC.wait(S.mm1, gq_hist[gidx])
                if gidx - 2 >= 0:
                    SC.wait(S.sp1)
                for tl, ti in enumerate(tiles):
                    sslice = sst[:, (par * cfg.TG + tl) * cfg.W1:
                                 (par * cfg.TG + tl) * cfg.W1 + W]
                    SC.run("activation", out=sslice, in_=psum[par * 4 + 2 * tl][:, 0:W],
                           func=AF.Silu, sem=S.e1a)
                ev_a = S.e1a.v
                VE.wait(S.e1a, ev_a)
                if gidx - 2 >= 0:
                    VE.wait(S.sp1)
                for tl, ti in enumerate(tiles):
                    sslice = sst[:, (par * cfg.TG + tl) * cfg.W1:
                                 (par * cfg.TG + tl) * cfg.W1 + W]
                    supslice = supst[:, (par * cfg.TG + tl) * cfg.W1:
                                     (par * cfg.TG + tl) * cfg.W1 + W]
                    VE.run("tensor_tensor", out=supslice, in0=sslice,
                           in1=psum[par * 4 + 2 * tl + 1][:, 0:W], op=aop.mult,
                           sem=S.e1v)
                ev_v = S.e1v.v
                ev_hist[gidx] = (ev_a, ev_v)
                SY.wait(S.e1a, ev_a)
                SY.wait(S.e1v, ev_v)
                for tl, ti in enumerate(tiles):
                    sslice = sst[:, (par * cfg.TG + tl) * cfg.W1:
                                 (par * cfg.TG + tl) * cfg.W1 + W]
                    supslice = supst[:, (par * cfg.TG + tl) * cfg.W1:
                                     (par * cfg.TG + tl) * cfg.W1 + W]
                    SY.dma(S.sp1, s_d[ti, :, poff:poff + W], sslice)
                    SY.dma(S.sp1, sup_d[ti, :, poff:poff + W], supslice)
                sp_hist[gidx] = S.sp1.v
                gidx += 1
                while deq_i[0] * ngroups_total < NBAT * gidx:
                    deq_step()
        while deq_i[0] < NBAT:
            deq_step()
    flush_block(nc, "p1", qs)

    qs = new_queues()
    IH = (I + 1) // 2
    QH = min(1024, I)
    W2 = cfg.W2
    with (nc.sbuf_tensor("abufA", [128, I], F16) as abufA,
          nc.sbuf_tensor("abufD", [128, I], F16) as abufD,
          nc.sbuf_tensor("junkA", [128, IH], F8) as junkA,
          nc.sbuf_tensor("junkD", [128, IH], F8) as junkD,
          nc.sbuf_tensor("paq", [128, 2 * QH], F16) as paq,
          nc.sbuf_tensor("psq", [128, 2 * QH], BF16) as psq,
          nc.sbuf_tensor("pmk", [128, QH], BF16) as pmk,
          nc.sbuf_tensor("wqr", [128, 2 * 2 * W2], BF16) as wqr,
          nc.sbuf_tensor("gst", [128, 2 * W2], F32) as gst,
          contextlib.ExitStack() as pst):
        psum = [pst.enter_context(nc.psum_tensor(f"p2_{i}", [128, 512], F32))
                for i in range(8)]
        SY, SC, VE, GP, TE = (qs[n] for n in ("sync", "scalar", "vector", "gpsimd", "tensor"))
        for q in (SY, SC, VE, GP):
            q.wait(S.sp1)
            q.wait(S.wqsp)
        TE.wait(S.trx)

        US_MM = 0.00022 * 512
        tPE = 0.0
        evac_pts = []
        peb_hist = {}
        NHALF = cfg.TH2
        halves = [list(range(h * 8, min(TT, h * 8 + 8))) for h in range(NHALF)]
        nsilu_per = [len(h) for h in halves]
        evg_after = {}
        cum = 0
        for op_ in range(cfg.OP):
            for th_ in range(NHALF):
                cum += nsilu_per[th_]
                evg_after[(op_, th_)] = cum
        for op_ in range(cfg.OP):
            ocols = slice(op_ * W2, (op_ + 1) * W2)
            for th_ in range(NHALF):
                tiles = halves[th_]
                for b0 in range(0, BLK, 2):
                    gpar = (b0 // 2) % 2
                    seqg = (op_ * NHALF + th_) * (BLK // 2) + b0 // 2
                    if seqg - 2 >= 0:
                        GP.wait(S.peb, peb_hist[seqg - 2])
                    GP.dma(wqld2[gpar],
                           wqr[:, gpar * 2 * W2:(gpar + 1) * 2 * W2].rearrange(
                               "p (b w) -> p b w", b=2),
                           wq_dram[b0:b0 + 2, :, ocols].rearrange("b p w -> p b w"))
                    TE.wait(wqld2[gpar])
                    if op_ > 0 and b0 == 0 and th_ == 0:
                        TE.wait(S.evg, evg_after[(op_ - 1, NHALF - 1)])
                    if th_ == 1 and b0 == 0:
                        TE.wait(S.evg, evg_after[(op_, 0)])
                    for bl in range(2):
                        b = b0 + bl
                        rhs = wqr[:, (gpar * 2 + bl) * W2:(gpar * 2 + bl + 1) * W2]
                        for n_, ti in enumerate(tiles):
                            lhs = xnT[:, b * T + ti * 128: b * T + (ti + 1) * 128]
                            pq = psum[n_][:, 0:W2]
                            TE.run("matmul", pq, lhs, rhs,
                                   start=(b == 0), stop=(b == BLK - 1),
                                   sem=(S.peb if (bl == 1 and n_ == len(tiles) - 1)
                                        else None))
                        tPE += len(tiles) * US_MM
                    peb_hist[seqg] = S.peb.v
                evac_pts.append((S.peb.v, op_, th_, tiles, tPE))

        alltiles = list(range(TT))
        NA = min((9 * TT + 15) // 16, TT)
        tiles_A = alltiles[:NA]
        tiles_D = alltiles[NA:]
        Kh = cfg.K + 0.5
        VE.run("memset", th_a[:, :], cfg.TH0, sem=S.chD)
        th_bufs = [th_a, th_b]
        chA_after = {}
        opsA = 1 + 5 * cfg.NIT
        for n in range(len(tiles_A)):
            chA_after[n] = (n + 1) * opsA
        sldA_hist = {}
        aspA_hist = {}
        aspD_hist = {}
        chD_after = {}
        hq_seq = [0]
        hq_tt = {}

        def bisect_D(j, abuf, jbuf):
            VE.wait(sld2["D"])
            VE.run("scalar_tensor_tensor", out=abuf, in0=abuf, scalar=-1.0,
                   in1=abuf, op0=aop.mult, op1=aop.max, sem=S.chD)
            w = cfg.W0
            for it in range(cfg.NIT):
                cur = th_bufs[it % 2]
                nxt = th_bufs[(it + 1) % 2]
                VE.wait(S.chD)
                VE.run("tensor_scalar", out=jbuf, in0=abuf[:, 0:IH],
                       scalar1=cur[:, j:j + 1], scalar2=None, op0=aop.is_gt,
                       op1=aop.add, accum_out=cnt_t[:, j:j + 1], sem=S.chD)
                VE.wait(S.chD)
                VE.run("tensor_scalar", out=jbuf[:, 0:I - IH], in0=abuf[:, IH:I],
                       scalar1=cur[:, j:j + 1], scalar2=None, op0=aop.is_gt,
                       op1=aop.add, accum_out=sgn_t[:, j:j + 1], sem=S.chD)
                VE.wait(S.chD)
                VE.run("tensor_tensor", out=cnt_t[:, j:j + 1],
                       in0=cnt_t[:, j:j + 1], in1=sgn_t[:, j:j + 1],
                       op=aop.add, sem=S.chD)
                VE.wait(S.chD)
                VE.run("tensor_scalar", out=sgn_t[:, j:j + 1],
                       in0=cnt_t[:, j:j + 1], scalar1=float(Kh), scalar2=None,
                       op0=aop.is_gt, sem=S.chD)
                VE.wait(S.chD)
                VE.run("tensor_scalar", out=sgn_t[:, j:j + 1],
                       in0=sgn_t[:, j:j + 1], scalar1=float(2 * w),
                       scalar2=float(w), op0=aop.mult, op1=aop.subtract,
                       sem=S.chD)
                VE.wait(S.chD)
                VE.run("tensor_tensor", out=nxt[:, j:j + 1],
                       in0=cur[:, j:j + 1], in1=sgn_t[:, j:j + 1],
                       op=aop.add, sem=S.chD)
                w *= 0.5

        NT = max(len(tiles_A), len(tiles_D))
        for n in range(NT):
            if n < len(tiles_A):
                j = tiles_A[n]
                if n - 1 >= 0:
                    SY.wait(asp2["A"])
                    SY.wait(S.chA, chA_after[n - 1])
                SY.dma(sld2["A"], abufA, s_d[j, :, :])
                sldA_hist[n] = sld2["A"].v
                SY.wait(S.chA, n * opsA + 1)
                SY.dma(asp2["A"], s_d[j, :, :], abufA)
                aspA_hist[n] = asp2["A"].v
            if n < len(tiles_D):
                j = tiles_D[n]
                if n - 1 >= 0:
                    SY.wait(asp2["D"])
                    SY.wait(S.chD, S.chD.v)
                SY.dma(sld2["D"], abufD, s_d[j, :, :])
                bisect_D(j, abufD, junkD)
                SY.wait(S.chD, S.chD.v - 6 * cfg.NIT)
                SY.dma(asp2["D"], s_d[j, :, :], abufD)
                chD_after[n] = S.chD.v
                aspD_hist[n] = asp2["D"].v

        a_steps = []
        for n, j in enumerate(tiles_A):

            def start_step(n=n, j=j):
                SC.wait(sld2["A"], sldA_hist[n])
                SC.wait(S.chD, 1)
                SC.run("activation", out=abufA, in_=abufA, func=AF.Abs, sem=S.chA)
            a_steps.append((10.0, start_step))
            w = cfg.W0
            for it in range(cfg.NIT):
                cur = th_bufs[it % 2]
                nxt = th_bufs[(it + 1) % 2]

                def iter_step(j=j, cur=cur, nxt=nxt, w=w):
                    SC.wait(S.chA)
                    SC.run("activation", out=junkA, in_=abufA[:, 0:IH],
                           func=AF.Sign, bias=cur[:, j:j + 1], scale=-1.0,
                           accum_out=cnt_t[:, j:j + 1], sem=S.chA)
                    SC.wait(S.chA)
                    SC.run("activation", out=junkA[:, 0:I - IH],
                           in_=abufA[:, IH:I], func=AF.Sign,
                           bias=cur[:, j:j + 1], scale=-1.0,
                           accum_out=sgn_t[:, j:j + 1], sem=S.chA)
                    SC.wait(S.chA)
                    SC.run("activation", out=cnt_t[:, j:j + 1],
                           in_=cnt_t[:, j:j + 1], func=AF.Identity, scale=1.0,
                           bias=sgn_t[:, j:j + 1], sem=S.chA)
                    SC.wait(S.chA)
                    SC.run("activation", out=sgn_t[:, j:j + 1],
                           in_=cnt_t[:, j:j + 1], func=AF.Sign, scale=-1.0,
                           bias=0.0, sem=S.chA)
                    SC.wait(S.chA)
                    SC.run("activation", out=nxt[:, j:j + 1],
                           in_=sgn_t[:, j:j + 1], func=AF.Identity, scale=float(w),
                           bias=cur[:, j:j + 1], sem=S.chA)
                a_steps.append((10.8, iter_step))
                w *= 0.5

        fin_th = th_bufs[cfg.NIT % 2]
        jobs = []
        for n in range(max(len(tiles_A), len(tiles_D))):
            if n < len(tiles_A):
                jobs.append((tiles_A[n], S.chA, chA_after[n],
                             asp2["A"], aspA_hist[n]))
            if n < len(tiles_D):
                jobs.append((tiles_D[n], S.chD, chD_after[n],
                             asp2["D"], aspD_hist[n]))
        useq = [0]
        u_tt = {}
        for (j, chs, chv, asps, aspv) in jobs:
            GP.wait(chs, chv)
            GP.wait(asps, aspv)
            for c0 in range(0, I, QH):
                cw = min(QH, I - c0)
                u = useq[0]
                par = u % 2
                aq = paq[:, par * QH: par * QH + cw]
                sq = psq[:, par * QH: par * QH + cw]
                if u - 2 >= 0:
                    GP.wait(S.mk, u_tt[u - 2])
                    GP.wait(S.hmsp)
                GP.dma(hmld2[par], aq, s_d[j, :, c0:c0 + cw])
                GP.dma(hmld2[par], sq, sup_d[j, :, c0:c0 + cw])
                GP.wait(hmld2[par])
                GP.wait(S.mk)
                GP.run("tensor_scalar", out=pmk[:, 0:cw], in0=aq,
                       scalar1=fin_th[:, j:j + 1], scalar2=None,
                       op0=aop.is_gt, sem=S.mk)
                GP.wait(S.mk)
                GP.run("tensor_tensor", out=sq, in0=pmk[:, 0:cw], in1=sq,
                       op=aop.mult, sem=S.mk)
                u_tt[u] = S.mk.v
                GP.wait(S.mk, u_tt[u])
                GP.dma(S.hmsp, sup_d[j, :, c0:c0 + cw], sq)
                useq[0] += 1
        GP.wait(S.hmsp)
        ai = 0
        tACT = 0.0
        gs_seq = [0]
        for (peb_tgt, op_, th_, tiles, t_ready) in evac_pts:
            while ai < len(a_steps) and tACT + a_steps[ai][0] <= t_ready:
                tACT += a_steps[ai][0]
                a_steps[ai][1]()
                ai += 1
            SC.wait(S.peb, peb_tgt)
            ocols = slice(op_ * W2, (op_ + 1) * W2)
            for n_, ti in enumerate(tiles):
                sl = gs_seq[0] % 2
                if gs_seq[0] - 2 >= 0:
                    SC.wait(gsp2[sl])
                SC.run("activation", out=gst[:, sl * W2:(sl + 1) * W2],
                       in_=psum[n_][:, 0:W2], func=AF.Silu, sem=S.evg)
                SC.wait(S.evg, S.evg.v)
                SC.dma(gsp2[sl], g_d[ti * 128:(ti + 1) * 128, ocols],
                       gst[:, sl * W2:(sl + 1) * W2])
                gs_seq[0] += 1
            tACT = max(tACT, t_ready) + len(tiles) * 1.5
        while ai < len(a_steps):
            a_steps[ai][1]()
            ai += 1
        chain_done = (S.chA.v, S.chD.v)
    flush_block(nc, "p2p3a", qs)

    xnT_ctx.__exit__(None, None, None)

    qs = new_queues()
    NGRP = len(cfg.groups3)
    NHP = len(cfg.hpanels)
    HPW = cfg.HP
    supflat = sup_d.rearrange("a b c -> (a b) c")
    with (nc.sbuf_tensor("hmT", [128, 2 * cfg.G3 * T], BF16) as hmT,
          nc.sbuf_tensor("wdr", [128, 2 * cfg.G3 * HPW], BF16) as wdr,
          nc.sbuf_tensor("outs", [128, TT * HPW], F32) as outs,
          nc.sbuf_tensor("xg", [128, 4 * HPW], F32) as xg,
          contextlib.ExitStack() as pst):
        psum = [pst.enter_context(nc.psum_tensor(f"p3_{i}", [128, 1024], F32))
                for i in range(4)]
        SY, SC, VE, GP, TE = (qs[n] for n in ("sync", "scalar", "vector", "gpsimd", "tensor"))
        SCQ = qs["scalar"]
        for q in (SY, SCQ, GP, VE):
            q.wait(S.sp1)
            q.wait(gsp2[0])
            q.wait(gsp2[1])
            q.wait(sld2["A"])
            q.wait(sld2["D"])
            q.wait(asp2["A"])
            q.wait(asp2["D"])
            q.wait(S.hmsp)
            q.wait(hmld2[0])
            q.wait(hmld2[1])

        fl_hist = {}
        mm3_hist = {}
        xg_hist = {}
        for hpi, (hoff, hw) in enumerate(cfg.hpanels):
            for gi, (ic0, gn) in enumerate(cfg.groups3):
                gseq = hpi * NGRP + gi
                par = gseq % 2
                prev = divmod(gseq - 2, NGRP) if gseq - 2 >= 0 else None
                if prev is not None and (prev[0], prev[1], TT - 1) in fl_hist:
                    pv = fl_hist[(prev[0], prev[1], TT - 1)]
                    SCQ.wait(S.fl, pv)
                    SY.wait(S.fl, pv)
                    GP.wait(S.fl, pv)
                if hpi == 0:
                    SY.wait(S.hmtt)
                for c in range(gn):
                    ic = ic0 + c
                    slot = hmT[:, (par * cfg.G3 + c) * T:
                               (par * cfg.G3 + c) * T + T]
                    if hpi == 0:
                        SCQ.dma(trh2[par], slot,
                                supflat[:, ic * 128:(ic + 1) * 128],
                                transpose=True)
                    else:
                        SCQ.dma(trh2[par], slot, hm_d[ic * 128:(ic + 1) * 128, :])
                    GP.dma(wdld2[par], wdr[:, (par * cfg.G3 + c) * HPW:
                                           (par * cfg.G3 + c) * HPW + hw],
                           wdt_d[ic, :, hoff:hoff + hw])
                TE.wait(trh2[par])
                TE.wait(wdld2[par])
                if hpi == 0:
                    SY.wait(trh2[par])
                    for c in range(gn):
                        ic = ic0 + c
                        slot = hmT[:, (par * cfg.G3 + c) * T:
                                   (par * cfg.G3 + c) * T + T]
                        SY.dma(S.hmtt, hm_d[ic * 128:(ic + 1) * 128, :], slot)
                for ti in range(TT):
                    pp = psum[ti % 4]
                    key = (hpi, gi, ti)
                    pk = _prev_psum_use(key, NGRP, TT)
                    if pk is not None and pk in fl_hist:
                        TE.wait(S.fl, fl_hist[pk])
                    nsub = (hw + 511) // 512
                    for c in range(gn):
                        lhs = hmT[:, (par * cfg.G3 + c) * T + ti * 128:
                                  (par * cfg.G3 + c) * T + (ti + 1) * 128]
                        for si, sub in enumerate(range(0, hw, 512)):
                            sw = min(512, hw - sub)
                            last = (c == gn - 1 and si == nsub - 1)
                            TE.run("matmul", pp[:, sub:sub + sw], lhs,
                                   wdr[:, (par * cfg.G3 + c) * HPW + sub:
                                       (par * cfg.G3 + c) * HPW + sub + sw],
                                   start=(c == 0), stop=(c == gn - 1),
                                   sem=(S.mm1 if last else None))
                    mm3_hist[(hpi, gi, ti)] = S.mm1.v
                    VE.wait(S.mm1, mm3_hist[(hpi, gi, ti)])
                    od = outs[:, ti * HPW: ti * HPW + hw]
                    if gi == 0:
                        if hpi > 0:
                            VE.wait(S.fl, fl_hist[(hpi - 1, NGRP - 1, ti)])
                            VE.wait(S.outp)
                        VE.run("tensor_copy", od, pp[:, 0:hw], sem=S.fl)
                    else:
                        VE.wait(S.fl, fl_hist[(hpi, gi - 1, ti)])
                        VE.run("tensor_tensor", out=od, in0=od, in1=pp[:, 0:hw],
                               op=aop.add, sem=S.fl)
                    fl_hist[(hpi, gi, ti)] = S.fl.v
                    if gi == NGRP - 1:
                        sl = ti % 2
                        xs = xg[:, sl * HPW: sl * HPW + hw]
                        gs_ = xg[:, (2 + sl) * HPW: (2 + sl) * HPW + hw]
                        if (hpi, ti - 2) in xg_hist:
                            SY.wait(S.fl, xg_hist[(hpi, ti - 2)])
                        elif hpi > 0 and (hpi - 1, ti + TT - 2) in xg_hist:
                            SY.wait(S.fl, xg_hist[(hpi - 1, ti + TT - 2)])
                        SY.dma(xgld2[sl], xs,
                               x_d[ti * 128:(ti + 1) * 128, hoff:hoff + hw])
                        SY.dma(xgld2[sl], gs_,
                               g_d[ti * 128:(ti + 1) * 128, hoff:hoff + hw])
                        VE.wait(xgld2[sl])
                        VE.wait(S.fl)
                        VE.run("tensor_tensor", out=od, in0=od, in1=gs_,
                               op=aop.mult, sem=S.fl)
                        VE.wait(S.fl)
                        VE.run("tensor_tensor", out=od, in0=od, in1=xs,
                               op=aop.add, sem=S.fl)
                        xg_hist[(hpi, ti)] = S.fl.v
                        fl_hist[(hpi, gi, ti)] = S.fl.v
                        SY.wait(S.fl, S.fl.v)
                        SY.dma(S.outp, out_d[ti * 128:(ti + 1) * 128, hoff:hoff + hw],
                               od)
        SY.wait(S.hmtt)
        SY.wait(S.outp, S.outp.v)
        qs["gpsimd"].wait(S.outp, S.outp.v)
    flush_block(nc, "p3b", qs)

    st.close()
    return nc


def _prev_psum_use(key, ngrp, tt):
    hpi, gi, ti = key
    ti2 = ti - 4
    if ti2 >= 0:
        return (hpi, gi, ti2)
    gi2 = gi - 1
    if gi2 >= 0:
        return (hpi, gi2, tt + ti2)
    if hpi > 0:
        return (hpi - 1, ngrp - 1, tt + ti2)
    return None


def _host_pack(cfg, w_gate, w_up, w_down, packed, scales):
    H, I, O, KC, BLK, IC = cfg.H, cfg.I, cfg.O, cfg.KC, cfg.BLK, cfg.IC
    blockperm = np.array([2 * (p % 64) + (p // 64) for p in range(128)])
    sig = (np.arange(H) // 128) * 128 + blockperm[np.arange(H) % 128]

    def pack_w(w):
        wt = np.ascontiguousarray(w.T[sig])
        pieces = []
        for (poff, W) in cfg.panels1:
            blk = wt[:, poff:poff + W].reshape(KC, 128, W)
            for g in range(cfg.NGR):
                ck = min(cfg.GK, KC - g * cfg.GK)
                pieces.append(np.ascontiguousarray(blk[g * cfg.GK:g * cfg.GK + ck]).ravel())
        return np.concatenate(pieces)

    wg_flat = pack_w(np.asarray(w_gate, np.float32))
    wu_flat = pack_w(np.asarray(w_up, np.float32))
    wdt = np.ascontiguousarray(np.asarray(w_down, np.float32).T).reshape(IC, 128, O)
    pkt = np.ascontiguousarray(np.asarray(packed, np.int32).transpose(1, 2, 0))
    sc = np.asarray(scales, np.float32)[:, :, 0].T
    scb = np.ascontiguousarray(np.broadcast_to(sc[:, None, :], (BLK, 128, O)))
    return wg_flat, wu_flat, wdt, pkt, scb


def kernel(L0_attn_x, L0_attn_w_gate, L0_attn_w_up, L0_attn_w_down,
           L0_ffn_packed_weight, L0_ffn_scales):
    cfg = Cfg()
    x = np.asarray(L0_attn_x, np.float32)
    B = x.shape[0]
    wg_flat, wu_flat, wdt, pkt, scb = _host_pack(
        cfg, L0_attn_w_gate, L0_attn_w_up, L0_attn_w_down,
        L0_ffn_packed_weight, L0_ffn_scales)
    nc = build_nc(cfg)
    in_maps = []
    for c in range(B):
        in_maps.append(dict(x=np.ascontiguousarray(x[c]), wg=wg_flat, wu=wu_flat,
                            wdt=wdt, pkt=pkt, scb=scb))
    res = run_bass_kernel_spmd(nc, in_maps, core_ids=list(range(B)))
    out = np.stack([np.asarray(r["out"], np.float32) for r in res.results])
    return out


# version 10
# speedup vs baseline: 1.0499x; 1.0499x over previous
import contextlib
import numpy as np
import ml_dtypes
import concourse.bass as bass
import concourse.mybir as mybir
from concourse.bass_utils import run_bass_kernel_spmd
from concourse.alu_op_type import AluOpType as aop

F32 = mybir.dt.float32
F16 = mybir.dt.float16
BF16 = mybir.dt.bfloat16
I32 = mybir.dt.int32
F8 = mybir.dt.float8e4
AF = mybir.ActivationFunctionType
AX = mybir.AxisListType
bfnp = ml_dtypes.bfloat16


class Cfg:
    def __init__(self, T=2048, H=4096, I=11008, O=4096, NIT=15, n_cores=8):
        self.T, self.H, self.I, self.O = T, H, I, O
        self.NIT = NIT
        self.n_cores = n_cores
        self.TT = T // 128
        self.KC = H // 128
        self.IC = I // 128
        self.BLK = H // 128
        self.K = I // 2
        self.W1 = 384
        ws = []
        off = 0
        while off < I:
            w = min(self.W1, I - off)
            ws.append((off, w))
            off += w
        self.panels1 = ws
        self.GK = 4
        self.NGR = (self.KC + self.GK - 1) // self.GK
        self.TG = 2
        assert self.TT % self.TG == 0
        assert self.TT <= 16
        self.W2 = 512 if O >= 512 else O
        assert O % self.W2 == 0
        self.OP = O // self.W2
        self.TH2 = (self.TT + 7) // 8
        self.BB = 2
        assert self.BLK % self.BB == 0
        self.G3 = 8
        gs = []
        ic = 0
        while ic < self.IC:
            g = min(self.G3, self.IC - ic)
            gs.append((ic, g))
            ic += g
        self.groups3 = gs
        self.HP = 1024 if O >= 1024 else O
        hp = []
        off = 0
        while off < O:
            w = min(self.HP, O - off)
            hp.append((off, w))
            off += w
        self.hpanels = hp
        self.TH0 = 8.0
        self.W0 = 4.0


class Sem:
    def __init__(self, st, nc, name):
        self.h = st.enter_context(nc.semaphore(name))
        self.v = 0


class Q:

    def __init__(self, name):
        self.name = name
        self.ops = []
        self._lastwait = {}

    def add(self, f):
        self.ops.append(f)

    def wait(self, sem, val=None):
        v = sem.v if val is None else val
        if v <= 0:
            return
        if self._lastwait.get(id(sem), -1) >= v:
            return
        self._lastwait[id(sem)] = v
        self.ops.append(lambda e, h=sem.h, v=v: e.wait_ge(h, v))

    @staticmethod
    def _ap(x):
        if isinstance(x, bass.AP) or not hasattr(x, "ap"):
            return x
        return x.ap()

    def dma(self, sem, out, in_, **kw):
        sem.v += 16
        out, in_ = self._ap(out), self._ap(in_)
        def f(e, out=out, in_=in_, kw=kw, h=sem.h):
            e.dma_start(out=out, in_=in_, **kw).then_inc(h, 16)
        self.ops.append(f)

    def run(self, fname, *args, sem=None, inc=1, **kw):
        if sem is not None:
            sem.v += inc
        args = tuple(self._ap(a) for a in args)
        kw = {k: (self._ap(v) if k in ("out", "in_", "in0", "in1") else v)
              for k, v in kw.items()}
        def f(e, fname=fname, args=args, kw=kw, sem=sem, inc=inc):
            r = getattr(e, fname)(*args, **kw)
            if sem is not None:
                r.then_inc(sem.h, inc)
        self.ops.append(f)


def flush_block(nc, name, qs):
    with nc.Block(name) as blk:
        for q in qs.values():
            if q.ops:
                runner = getattr(blk, q.name)
                def body(eng, ops=q.ops):
                    for f in ops:
                        f(eng)
                runner(body)


def new_queues():
    return {n: Q(n) for n in ("sync", "scalar", "vector", "gpsimd", "tensor")}


def build_nc(cfg: Cfg):
    T, H, I, O = cfg.T, cfg.H, cfg.I, cfg.O
    TT, KC, IC, BLK = cfg.TT, cfg.KC, cfg.IC, cfg.BLK
    nc = bass.Bass("TRN2", target_bir_lowering=False)
    st = contextlib.ExitStack()

    x_d = nc.dram_tensor("x", [T, H], F32, kind="ExternalInput").ap()
    ng = 4 * 128
    wg_sz = H * I
    wg_d = nc.dram_tensor("wg", [wg_sz], F32, kind="ExternalInput").ap()
    wu_d = nc.dram_tensor("wu", [wg_sz], F32, kind="ExternalInput").ap()
    wdt_d = nc.dram_tensor("wdt", [IC, 128, O], F32, kind="ExternalInput").ap()
    pkt_d = nc.dram_tensor("pkt", [BLK, 64, O], I32, kind="ExternalInput").ap()
    scb_d = nc.dram_tensor("scb", [BLK, 128, O], F32, kind="ExternalInput").ap()
    out_d = nc.dram_tensor("out", [T, O], F32, kind="ExternalOutput").ap()
    xn_d = nc.dram_tensor("xn_s", [T, H], BF16, kind="Internal").ap()
    s_d = nc.dram_tensor("s_s", [TT, 128, I], F16, kind="Internal").ap()
    sup_d = nc.dram_tensor("sup_s", [TT, 128, I], BF16, kind="Internal").ap()
    mask_d = nc.dram_tensor("mask_s", [TT, 128, I], F8, kind="Internal").ap()
    hm_d = nc.dram_tensor("hm_s", [I, T], BF16, kind="Internal").ap()
    g_d = nc.dram_tensor("g_s", [T, O], F32, kind="Internal").ap()
    wq_dram = nc.dram_tensor("wq_s", [BLK, 128, O], BF16, kind="Internal").ap()

    gran_off = {}
    off = 0
    for p, (poff, W) in enumerate(cfg.panels1):
        for g in range(cfg.NGR):
            ck = min(cfg.GK, KC - g * cfg.GK)
            gran_off[(p, g)] = (off, ck, W)
            off += ck * 128 * W
    assert off == wg_sz

    sems = {}
    for name in ("xld", "v0", "a0", "spxn", "trx",
                 "pegr", "mm1", "e1a", "e1v", "sp1",
                 "pkld", "scld", "dq", "evg", "gsp", "peb",
                 "sld", "chA", "chD", "mk", "msp",
                 "hmld", "hmtt", "hmsp", "trh", "wdld", "xgld", "fl", "outp",
                 "wqsp"):
        sems[name] = Sem(st, nc, name)
    S = type("S", (), sems)
    wld = [Sem(st, nc, f"wld{g}") for g in range(cfg.NGR)]
    xld2 = [Sem(st, nc, f"xld{i}") for i in range(2)]
    pkld2 = [Sem(st, nc, f"pkld{i}") for i in range(2)]
    scld2 = [Sem(st, nc, f"scld{i}") for i in range(2)]
    sld2 = {c: Sem(st, nc, f"sld{c}") for c in "AD"}
    msp2 = {c: Sem(st, nc, f"msp{c}") for c in "AD"}
    hmld2 = [Sem(st, nc, f"hmld{i}") for i in range(2)]
    trh2 = [Sem(st, nc, f"trh{i}") for i in range(2)]
    wdld2 = [Sem(st, nc, f"wdld{i}") for i in range(2)]
    xgld2 = [Sem(st, nc, f"xgld{i}") for i in range(2)]
    wqld2 = [Sem(st, nc, f"wqld{i}") for i in range(2)]
    gsp2 = [Sem(st, nc, f"gsp{i}") for i in range(2)]
    asp2 = {c: Sem(st, nc, f"asp{c}") for c in "AD"}

    stats = st.enter_context(nc.sbuf_tensor("stats", [128, 16], F32))
    th_a = st.enter_context(nc.sbuf_tensor("th_a", [128, TT], F32))
    th_b = st.enter_context(nc.sbuf_tensor("th_b", [128, TT], F32))
    cnt_t = st.enter_context(nc.sbuf_tensor("cnt_t", [128, TT], F32))
    sgn_t = st.enter_context(nc.sbuf_tensor("sgn_t", [128, TT], F32))
    xnT_ctx = nc.sbuf_tensor("xnT", [128, KC * T], BF16)
    xnT = xnT_ctx.__enter__()

    qs = new_queues()
    with (nc.sbuf_tensor("xbuf", [128, 2 * H], F32) as xbuf,
          nc.sbuf_tensor("xnst", [128, 2 * H], BF16) as xnst,
          nc.sbuf_tensor("junk0", [128, H], F8) as junk8):
        SY, SC, VE = qs["sync"], qs["scalar"], qs["vector"]
        rel_x = {}
        for tt in range(TT):
            par = tt % 2
            xb = xbuf[:, par * H:(par + 1) * H]
            xs = xnst[:, par * H:(par + 1) * H]
            if tt - 2 >= 0:
                for sem, val in rel_x[tt - 2]:
                    SY.wait(sem, val)
            SY.dma(xld2[par], xb, x_d[tt * 128:(tt + 1) * 128, :])
            c = (tt % 2) * 8
            VE.wait(xld2[par])
            VE.run("reduce_sum", stats[:, c:c + 1], xb, axis=AX.X, sem=S.v0)
            VE.wait(S.v0)
            VE.run("tensor_scalar", out=stats[:, c + 1:c + 2], in0=stats[:, c:c + 1],
                   scalar1=-1.0 / H, scalar2=None, op0=aop.mult, sem=S.v0)
            SC.wait(S.v0)
            SC.wait(xld2[par], xld2[par].v)
            SC.run("activation", out=junk8[:, :H], in_=xb, func=AF.Square,
                   bias=stats[:, c + 1:c + 2], scale=1.0,
                   accum_out=stats[:, c + 2:c + 3], sem=S.a0)
            VE.wait(S.a0)
            VE.run("tensor_scalar", out=stats[:, c + 3:c + 4], in0=stats[:, c + 2:c + 3],
                   scalar1=1.0 / H, scalar2=1e-5, op0=aop.mult, op1=aop.add, sem=S.v0)
            SC.wait(S.v0)
            SC.run("activation", out=stats[:, c + 4:c + 5], in_=stats[:, c + 3:c + 4],
                   func=AF.Sqrt, sem=S.a0)
            VE.wait(S.a0)
            VE.run("reciprocal", stats[:, c + 5:c + 6], stats[:, c + 4:c + 5], sem=S.v0)
            VE.wait(S.v0)
            if tt - 2 >= 0:
                VE.wait(S.spxn)
            xs_perm = xs.rearrange("p (b r c) -> p b c r", b=KC, r=2)
            VE.run("tensor_scalar", out=xs_perm, in0=xb,
                   scalar1=stats[:, c + 1:c + 2], scalar2=stats[:, c + 5:c + 6],
                   op0=aop.add, op1=aop.mult, sem=S.v0)
            rel_x[tt] = [(S.a0, S.a0.v), (S.v0, S.v0.v)]
            SY.wait(S.v0)
            SY.dma(S.spxn, xn_d[tt * 128:(tt + 1) * 128, :], xs)
        SCQ = qs["scalar"]
        SCQ.wait(S.spxn)
        for k in range(KC):
            SCQ.dma(S.trx, xnT[:, k * T:(k + 1) * T],
                    xn_d[:, k * 128:(k + 1) * 128], transpose=True)
    flush_block(nc, "p0", qs)

    qs = new_queues()
    slotW = cfg.GK * cfg.W1
    with (nc.sbuf_tensor("wgr", [128, cfg.NGR * slotW], BF16) as wgr,
          nc.sbuf_tensor("wur", [128, cfg.NGR * slotW], BF16) as wur,
          nc.sbuf_tensor("sst", [128, 2 * cfg.TG * cfg.W1], F16) as sst,
          nc.sbuf_tensor("supst", [128, 2 * cfg.TG * cfg.W1], BF16) as supst,
          nc.sbuf_tensor("pkq", [128, 2 * 1024], I32) as pkq,
          nc.sbuf_tensor("nibq", [128, 1024], I32) as nibq,
          nc.sbuf_tensor("scq", [128, 2 * 1024], F32) as scq,
          nc.sbuf_tensor("wqst", [128, 2 * 1024], BF16) as wqst,
          contextlib.ExitStack() as pst):
        psum = [pst.enter_context(nc.psum_tensor(f"p1_{i}", [128, 512], F32))
                for i in range(8)]
        SY, SC, VE, GP, TE = (qs[n] for n in ("sync", "scalar", "vector", "gpsimd", "tensor"))
        TE.wait(S.trx)
        NGROUP = TT // cfg.TG
        npan = len(cfg.panels1)
        ev_hist = {}
        sp_hist = {}
        gq_hist = {}
        def load_gran(p, g):
            offs, ck, W = gran_off[(p, g)]
            if p > 0:
                if g < cfg.NGR - 1:
                    GP.wait(S.pegr, (p - 1) * (cfg.NGR - 1) + g + 1)
                else:
                    GP.wait(S.mm1, p * NGROUP)
            for (ring, src) in ((wgr, wg_d), (wur, wu_d)):
                dst = ring[:, g * slotW + 0: g * slotW + ck * W]
                dst = dst.rearrange("p (c w) -> p c w", c=ck)
                srcap = src[offs: offs + ck * 128 * W]
                srcap = srcap.rearrange("(c p w) -> p c w", c=ck, p=128)
                GP.dma(wld[g], dst, srcap)
        for g in range(cfg.NGR):
            load_gran(0, g)
        DCW = min(512, O)
        batches = [(b0, c0) for b0 in range(0, BLK, 2) for c0 in range(0, O, DCW)]
        NBAT = len(batches)
        dqb_hist = {}
        deq_i = [0]

        def deq_step():
            bi = deq_i[0]
            if bi >= NBAT:
                return
            deq_i[0] += 1
            b0, c0 = batches[bi]
            par = bi % 2
            bw2 = 2 * DCW
            if bi - 2 >= 0:
                SY.wait(S.dq, dqb_hist[bi - 2])
            src = pkt_d[b0:b0 + 2, :, c0:c0 + DCW].rearrange("b p w -> p b w")
            SY.dma(pkld2[par], pkq[0:64, par * bw2:(par + 1) * bw2].rearrange(
                "p (b w) -> p b w", b=2), src)
            SY.dma(pkld2[par], pkq[64:128, par * bw2:(par + 1) * bw2].rearrange(
                "p (b w) -> p b w", b=2), src)
            scsrc = scb_d[b0:b0 + 2, :, c0:c0 + DCW].rearrange("b p w -> p b w")
            SY.dma(scld2[par], scq[:, par * bw2:(par + 1) * bw2].rearrange(
                "p (b w) -> p b w", b=2), scsrc)
            VE.wait(pkld2[par])
            VE.wait(S.dq)
            VE.wait(S.wqsp)
            VE.run("tensor_scalar", out=nibq[0:64, :bw2],
                   in0=pkq[0:64, par * bw2:(par + 1) * bw2], scalar1=4,
                   scalar2=None, op0=aop.logical_shift_right, sem=S.dq)
            VE.run("tensor_scalar", out=nibq[64:128, :bw2],
                   in0=pkq[64:128, par * bw2:(par + 1) * bw2], scalar1=15,
                   scalar2=None, op0=aop.bitwise_and, sem=S.dq)
            VE.wait(scld2[par])
            VE.wait(S.dq)
            VE.run("scalar_tensor_tensor", out=wqst[:, par * bw2:(par + 1) * bw2],
                   in0=nibq[:, :bw2], scalar=8.0,
                   in1=scq[:, par * bw2:(par + 1) * bw2], op0=aop.subtract,
                   op1=aop.mult, sem=S.dq)
            dqb_hist[bi] = S.dq.v
            GP.wait(S.dq, dqb_hist[bi])
            GP.dma(S.wqsp, wq_dram[b0, :, c0:c0 + DCW],
                   wqst[:, par * bw2: par * bw2 + DCW])
            GP.dma(S.wqsp, wq_dram[b0 + 1, :, c0:c0 + DCW],
                   wqst[:, par * bw2 + DCW:(par + 1) * bw2])

        gidx = 0
        ngroups_total = len(cfg.panels1) * NGROUP
        for p, (poff, W) in enumerate(cfg.panels1):
            for tg in range(NGROUP):
                par = tg % 2
                tiles = (cfg.TG * tg, cfg.TG * tg + 1)
                if gidx - 2 >= 0:
                    ea, ev = ev_hist[gidx - 2]
                    TE.wait(S.e1a, ea)
                    TE.wait(S.e1v, ev)
                for k in range(KC):
                    g = k // cfg.GK
                    if tg == 0:
                        TE.wait(wld[g], 32 * (p + 1))
                    ck_local = k % cfg.GK
                    woff = g * slotW + ck_local * W
                    for tl, ti in enumerate(tiles):
                        lhs = xnT[:, k * T + ti * 128: k * T + (ti + 1) * 128]
                        pg = psum[par * 4 + 2 * tl][:, 0:W]
                        pu = psum[par * 4 + 2 * tl + 1][:, 0:W]
                        kw = dict(start=(k == 0), stop=(k == KC - 1))
                        last_of_group = (k == KC - 1 and tl == len(tiles) - 1)
                        rel_gran = (tg == NGROUP - 1 and ck_local == 0 and g > 0
                                    and tl == 0)
                        if rel_gran:
                            TE.run("matmul", pg, lhs, wgr[:, woff:woff + W],
                                   sem=S.pegr, **kw)
                            if p + 1 < npan:
                                load_gran(p + 1, g - 1)
                        else:
                            TE.run("matmul", pg, lhs, wgr[:, woff:woff + W], **kw)
                        if last_of_group:
                            TE.run("matmul", pu, lhs, wur[:, woff:woff + W],
                                   sem=S.mm1, **kw)
                        else:
                            TE.run("matmul", pu, lhs, wur[:, woff:woff + W], **kw)
                gq_hist[gidx] = S.mm1.v
                if tg == NGROUP - 1 and p + 1 < npan:
                    load_gran(p + 1, cfg.NGR - 1)
                SC.wait(S.mm1, gq_hist[gidx])
                if gidx - 2 >= 0:
                    SC.wait(S.sp1)
                for tl, ti in enumerate(tiles):
                    sslice = sst[:, (par * cfg.TG + tl) * cfg.W1:
                                 (par * cfg.TG + tl) * cfg.W1 + W]
                    SC.run("activation", out=sslice, in_=psum[par * 4 + 2 * tl][:, 0:W],
                           func=AF.Silu, sem=S.e1a)
                ev_a = S.e1a.v
                VE.wait(S.e1a, ev_a)
                if gidx - 2 >= 0:
                    VE.wait(S.sp1)
                for tl, ti in enumerate(tiles):
                    sslice = sst[:, (par * cfg.TG + tl) * cfg.W1:
                                 (par * cfg.TG + tl) * cfg.W1 + W]
                    supslice = supst[:, (par * cfg.TG + tl) * cfg.W1:
                                     (par * cfg.TG + tl) * cfg.W1 + W]
                    VE.run("tensor_tensor", out=supslice, in0=sslice,
                           in1=psum[par * 4 + 2 * tl + 1][:, 0:W], op=aop.mult,
                           sem=S.e1v)
                ev_v = S.e1v.v
                ev_hist[gidx] = (ev_a, ev_v)
                SY.wait(S.e1a, ev_a)
                SY.wait(S.e1v, ev_v)
                for tl, ti in enumerate(tiles):
                    sslice = sst[:, (par * cfg.TG + tl) * cfg.W1:
                                 (par * cfg.TG + tl) * cfg.W1 + W]
                    supslice = supst[:, (par * cfg.TG + tl) * cfg.W1:
                                     (par * cfg.TG + tl) * cfg.W1 + W]
                    SY.dma(S.sp1, s_d[ti, :, poff:poff + W], sslice)
                    SY.dma(S.sp1, sup_d[ti, :, poff:poff + W], supslice)
                sp_hist[gidx] = S.sp1.v
                gidx += 1
                while deq_i[0] * ngroups_total < NBAT * gidx:
                    deq_step()
        while deq_i[0] < NBAT:
            deq_step()
    flush_block(nc, "p1", qs)

    qs = new_queues()
    IH = (I + 1) // 2
    QH = min(1376, I)
    W2 = cfg.W2
    with (nc.sbuf_tensor("abufA", [128, I], F16) as abufA,
          nc.sbuf_tensor("abufD", [128, I], F16) as abufD,
          nc.sbuf_tensor("junkA", [128, IH], F8) as junkA,
          nc.sbuf_tensor("junkD", [128, IH], F8) as junkD,
          nc.sbuf_tensor("paq", [128, 2 * QH], F16) as paq,
          nc.sbuf_tensor("psq", [128, 2 * QH], BF16) as psq,
          nc.sbuf_tensor("pmk", [128, QH], BF16) as pmk,
          nc.sbuf_tensor("wqr", [128, 2 * 2 * W2], BF16) as wqr,
          nc.sbuf_tensor("gst", [128, 2 * W2], F32) as gst,
          contextlib.ExitStack() as pst):
        psum = [pst.enter_context(nc.psum_tensor(f"p2_{i}", [128, 512], F32))
                for i in range(8)]
        SY, SC, VE, GP, TE = (qs[n] for n in ("sync", "scalar", "vector", "gpsimd", "tensor"))
        for q in (SY, SC, VE, GP):
            q.wait(S.sp1)
            q.wait(S.wqsp)
        TE.wait(S.trx)

        US_MM = 0.00022 * 512
        tPE = 0.0
        evac_pts = []
        peb_hist = {}
        NHALF = cfg.TH2
        halves = [list(range(h * 8, min(TT, h * 8 + 8))) for h in range(NHALF)]
        nsilu_per = [len(h) for h in halves]
        evg_after = {}
        cum = 0
        for op_ in range(cfg.OP):
            for th_ in range(NHALF):
                cum += nsilu_per[th_]
                evg_after[(op_, th_)] = cum
        for op_ in range(cfg.OP):
            ocols = slice(op_ * W2, (op_ + 1) * W2)
            for th_ in range(NHALF):
                tiles = halves[th_]
                for b0 in range(0, BLK, 2):
                    gpar = (b0 // 2) % 2
                    seqg = (op_ * NHALF + th_) * (BLK // 2) + b0 // 2
                    if seqg - 2 >= 0:
                        GP.wait(S.peb, peb_hist[seqg - 2])
                    GP.dma(wqld2[gpar],
                           wqr[:, gpar * 2 * W2:(gpar + 1) * 2 * W2].rearrange(
                               "p (b w) -> p b w", b=2),
                           wq_dram[b0:b0 + 2, :, ocols].rearrange("b p w -> p b w"))
                    TE.wait(wqld2[gpar])
                    if op_ > 0 and b0 == 0 and th_ == 0:
                        TE.wait(S.evg, evg_after[(op_ - 1, NHALF - 1)])
                    if th_ == 1 and b0 == 0:
                        TE.wait(S.evg, evg_after[(op_, 0)])
                    for bl in range(2):
                        b = b0 + bl
                        rhs = wqr[:, (gpar * 2 + bl) * W2:(gpar * 2 + bl + 1) * W2]
                        for n_, ti in enumerate(tiles):
                            lhs = xnT[:, b * T + ti * 128: b * T + (ti + 1) * 128]
                            pq = psum[n_][:, 0:W2]
                            TE.run("matmul", pq, lhs, rhs,
                                   start=(b == 0), stop=(b == BLK - 1),
                                   sem=(S.peb if (bl == 1 and n_ == len(tiles) - 1)
                                        else None))
                        tPE += len(tiles) * US_MM
                    peb_hist[seqg] = S.peb.v
                evac_pts.append((S.peb.v, op_, th_, tiles, tPE))

        alltiles = list(range(TT))
        NA = min((9 * TT + 15) // 16, TT)
        tiles_A = alltiles[:NA]
        tiles_D = alltiles[NA:]
        Kh = cfg.K + 0.5
        VE.run("memset", th_a[:, :], cfg.TH0, sem=S.chD)
        th_bufs = [th_a, th_b]
        chA_after = {}
        opsA = 1 + 5 * cfg.NIT
        for n in range(len(tiles_A)):
            chA_after[n] = (n + 1) * opsA
        sldA_hist = {}
        aspA_hist = {}
        aspD_hist = {}
        chD_after = {}
        hq_seq = [0]
        hq_tt = {}

        def bisect_D(j, abuf, jbuf):
            VE.wait(sld2["D"])
            VE.run("scalar_tensor_tensor", out=abuf, in0=abuf, scalar=-1.0,
                   in1=abuf, op0=aop.mult, op1=aop.max, sem=S.chD)
            w = cfg.W0
            for it in range(cfg.NIT):
                cur = th_bufs[it % 2]
                nxt = th_bufs[(it + 1) % 2]
                VE.wait(S.chD)
                VE.run("tensor_scalar", out=jbuf, in0=abuf[:, 0:IH],
                       scalar1=cur[:, j:j + 1], scalar2=None, op0=aop.is_gt,
                       op1=aop.add, accum_out=cnt_t[:, j:j + 1], sem=S.chD)
                VE.wait(S.chD)
                VE.run("tensor_scalar", out=jbuf[:, 0:I - IH], in0=abuf[:, IH:I],
                       scalar1=cur[:, j:j + 1], scalar2=None, op0=aop.is_gt,
                       op1=aop.add, accum_out=sgn_t[:, j:j + 1], sem=S.chD)
                VE.wait(S.chD)
                VE.run("tensor_tensor", out=cnt_t[:, j:j + 1],
                       in0=cnt_t[:, j:j + 1], in1=sgn_t[:, j:j + 1],
                       op=aop.add, sem=S.chD)
                VE.wait(S.chD)
                VE.run("tensor_scalar", out=sgn_t[:, j:j + 1],
                       in0=cnt_t[:, j:j + 1], scalar1=float(Kh), scalar2=None,
                       op0=aop.is_gt, sem=S.chD)
                VE.wait(S.chD)
                VE.run("tensor_scalar", out=sgn_t[:, j:j + 1],
                       in0=sgn_t[:, j:j + 1], scalar1=float(2 * w),
                       scalar2=float(w), op0=aop.mult, op1=aop.subtract,
                       sem=S.chD)
                VE.wait(S.chD)
                VE.run("tensor_tensor", out=nxt[:, j:j + 1],
                       in0=cur[:, j:j + 1], in1=sgn_t[:, j:j + 1],
                       op=aop.add, sem=S.chD)
                w *= 0.5

        NT = max(len(tiles_A), len(tiles_D))
        for n in range(NT):
            if n < len(tiles_A):
                j = tiles_A[n]
                if n - 1 >= 0:
                    SY.wait(asp2["A"])
                    SY.wait(S.chA, chA_after[n - 1])
                SY.dma(sld2["A"], abufA, s_d[j, :, :])
                sldA_hist[n] = sld2["A"].v
                SY.wait(S.chA, n * opsA + 1)
                SY.dma(asp2["A"], s_d[j, :, :], abufA)
                aspA_hist[n] = asp2["A"].v
            if n < len(tiles_D):
                j = tiles_D[n]
                if n - 1 >= 0:
                    SY.wait(asp2["D"])
                    SY.wait(S.chD, S.chD.v)
                SY.dma(sld2["D"], abufD, s_d[j, :, :])
                bisect_D(j, abufD, junkD)
                SY.wait(S.chD, S.chD.v - 6 * cfg.NIT)
                SY.dma(asp2["D"], s_d[j, :, :], abufD)
                chD_after[n] = S.chD.v
                aspD_hist[n] = asp2["D"].v

        a_steps = []
        for n, j in enumerate(tiles_A):

            def start_step(n=n, j=j):
                SC.wait(sld2["A"], sldA_hist[n])
                SC.wait(S.chD, 1)
                SC.run("activation", out=abufA, in_=abufA, func=AF.Abs, sem=S.chA)
            a_steps.append((10.0, start_step))
            w = cfg.W0
            for it in range(cfg.NIT):
                cur = th_bufs[it % 2]
                nxt = th_bufs[(it + 1) % 2]

                def iter_step(j=j, cur=cur, nxt=nxt, w=w):
                    SC.wait(S.chA)
                    SC.run("activation", out=junkA, in_=abufA[:, 0:IH],
                           func=AF.Sign, bias=cur[:, j:j + 1], scale=-1.0,
                           accum_out=cnt_t[:, j:j + 1], sem=S.chA)
                    SC.wait(S.chA)
                    SC.run("activation", out=junkA[:, 0:I - IH],
                           in_=abufA[:, IH:I], func=AF.Sign,
                           bias=cur[:, j:j + 1], scale=-1.0,
                           accum_out=sgn_t[:, j:j + 1], sem=S.chA)
                    SC.wait(S.chA)
                    SC.run("activation", out=cnt_t[:, j:j + 1],
                           in_=cnt_t[:, j:j + 1], func=AF.Identity, scale=1.0,
                           bias=sgn_t[:, j:j + 1], sem=S.chA)
                    SC.wait(S.chA)
                    SC.run("activation", out=sgn_t[:, j:j + 1],
                           in_=cnt_t[:, j:j + 1], func=AF.Sign, scale=-1.0,
                           bias=0.0, sem=S.chA)
                    SC.wait(S.chA)
                    SC.run("activation", out=nxt[:, j:j + 1],
                           in_=sgn_t[:, j:j + 1], func=AF.Identity, scale=float(w),
                           bias=cur[:, j:j + 1], sem=S.chA)
                a_steps.append((10.8, iter_step))
                w *= 0.5

        fin_th = th_bufs[cfg.NIT % 2]
        jobs = []
        for n in range(max(len(tiles_A), len(tiles_D))):
            if n < len(tiles_A):
                jobs.append((tiles_A[n], S.chA, chA_after[n],
                             asp2["A"], aspA_hist[n]))
            if n < len(tiles_D):
                jobs.append((tiles_D[n], S.chD, chD_after[n],
                             asp2["D"], aspD_hist[n]))
        units = []
        for (j, chs, chv, asps, aspv) in jobs:
            for c0 in range(0, I, QH):
                units.append((j, c0, min(QH, I - c0), chs, chv, asps, aspv))
        u_tt = {}
        u_ld = {}

        def pool_load(u):
            j, c0, cw, chs, chv, asps, aspv = units[u]
            par = u % 2
            GP.wait(chs, chv)
            GP.wait(asps, aspv)
            if u - 2 >= 0:
                GP.wait(S.mk, u_tt[u - 2])
                GP.wait(S.hmsp)
            GP.dma(hmld2[par], paq[:, par * QH: par * QH + cw],
                   s_d[j, :, c0:c0 + cw])
            GP.dma(hmld2[par], psq[:, par * QH: par * QH + cw],
                   sup_d[j, :, c0:c0 + cw])
            u_ld[u] = hmld2[par].v

        for u in range(min(2, len(units))):
            pool_load(u)
        for u in range(len(units)):
            j, c0, cw, chs, chv, asps, aspv = units[u]
            par = u % 2
            aq = paq[:, par * QH: par * QH + cw]
            sq = psq[:, par * QH: par * QH + cw]
            GP.wait(hmld2[par], u_ld[u])
            GP.wait(S.mk)
            GP.run("tensor_scalar", out=pmk[:, 0:cw], in0=aq,
                   scalar1=fin_th[:, j:j + 1], scalar2=None,
                   op0=aop.is_gt, sem=S.mk)
            GP.wait(S.mk)
            GP.run("tensor_tensor", out=sq, in0=pmk[:, 0:cw], in1=sq,
                   op=aop.mult, sem=S.mk)
            u_tt[u] = S.mk.v
            GP.wait(S.mk, u_tt[u])
            GP.dma(S.hmsp, sup_d[j, :, c0:c0 + cw], sq)
            if u + 2 < len(units):
                pool_load(u + 2)
        GP.wait(S.hmsp)
        ai = 0
        tACT = 0.0
        gs_seq = [0]
        for (peb_tgt, op_, th_, tiles, t_ready) in evac_pts:
            while ai < len(a_steps) and tACT + a_steps[ai][0] <= t_ready:
                tACT += a_steps[ai][0]
                a_steps[ai][1]()
                ai += 1
            SC.wait(S.peb, peb_tgt)
            ocols = slice(op_ * W2, (op_ + 1) * W2)
            for n_, ti in enumerate(tiles):
                sl = gs_seq[0] % 2
                if gs_seq[0] - 2 >= 0:
                    SC.wait(gsp2[sl])
                SC.run("activation", out=gst[:, sl * W2:(sl + 1) * W2],
                       in_=psum[n_][:, 0:W2], func=AF.Silu, sem=S.evg)
                SC.wait(S.evg, S.evg.v)
                SC.dma(gsp2[sl], g_d[ti * 128:(ti + 1) * 128, ocols],
                       gst[:, sl * W2:(sl + 1) * W2])
                gs_seq[0] += 1
            tACT = max(tACT, t_ready) + len(tiles) * 1.5
        while ai < len(a_steps):
            a_steps[ai][1]()
            ai += 1
        chain_done = (S.chA.v, S.chD.v)
    flush_block(nc, "p2p3a", qs)

    xnT_ctx.__exit__(None, None, None)

    qs = new_queues()
    NGRP = len(cfg.groups3)
    NHP = len(cfg.hpanels)
    HPW = cfg.HP
    supflat = sup_d.rearrange("a b c -> (a b) c")
    with (nc.sbuf_tensor("hmT", [128, 2 * cfg.G3 * T], BF16) as hmT,
          nc.sbuf_tensor("wdr", [128, 2 * cfg.G3 * HPW], BF16) as wdr,
          nc.sbuf_tensor("outs", [128, TT * HPW], F32) as outs,
          nc.sbuf_tensor("xg", [128, 4 * HPW], F32) as xg,
          contextlib.ExitStack() as pst):
        psum = [pst.enter_context(nc.psum_tensor(f"p3_{i}", [128, 1024], F32))
                for i in range(4)]
        SY, SC, VE, GP, TE = (qs[n] for n in ("sync", "scalar", "vector", "gpsimd", "tensor"))
        SCQ = qs["scalar"]
        for q in (SY, SCQ, GP, VE):
            q.wait(S.sp1)
            q.wait(gsp2[0])
            q.wait(gsp2[1])
            q.wait(sld2["A"])
            q.wait(sld2["D"])
            q.wait(asp2["A"])
            q.wait(asp2["D"])
            q.wait(S.hmsp)
            q.wait(hmld2[0])
            q.wait(hmld2[1])

        fl_hist = {}
        mm3_hist = {}
        xg_hist = {}
        for hpi, (hoff, hw) in enumerate(cfg.hpanels):
            for gi, (ic0, gn) in enumerate(cfg.groups3):
                gseq = hpi * NGRP + gi
                par = gseq % 2
                prev = divmod(gseq - 2, NGRP) if gseq - 2 >= 0 else None
                if prev is not None and (prev[0], prev[1], TT - 1) in fl_hist:
                    pv = fl_hist[(prev[0], prev[1], TT - 1)]
                    SCQ.wait(S.fl, pv)
                    SY.wait(S.fl, pv)
                    GP.wait(S.fl, pv)
                if hpi == 0:
                    SY.wait(S.hmtt)
                for c in range(gn):
                    ic = ic0 + c
                    slot = hmT[:, (par * cfg.G3 + c) * T:
                               (par * cfg.G3 + c) * T + T]
                    if hpi == 0:
                        SCQ.dma(trh2[par], slot,
                                supflat[:, ic * 128:(ic + 1) * 128],
                                transpose=True)
                    else:
                        SCQ.dma(trh2[par], slot, hm_d[ic * 128:(ic + 1) * 128, :])
                    GP.dma(wdld2[par], wdr[:, (par * cfg.G3 + c) * HPW:
                                           (par * cfg.G3 + c) * HPW + hw],
                           wdt_d[ic, :, hoff:hoff + hw])
                TE.wait(trh2[par])
                TE.wait(wdld2[par])
                if hpi == 0:
                    SY.wait(trh2[par])
                    for c in range(gn):
                        ic = ic0 + c
                        slot = hmT[:, (par * cfg.G3 + c) * T:
                                   (par * cfg.G3 + c) * T + T]
                        SY.dma(S.hmtt, hm_d[ic * 128:(ic + 1) * 128, :], slot)
                for ti in range(TT):
                    pp = psum[ti % 4]
                    key = (hpi, gi, ti)
                    pk = _prev_psum_use(key, NGRP, TT)
                    if pk is not None and pk in fl_hist:
                        TE.wait(S.fl, fl_hist[pk])
                    nsub = (hw + 511) // 512
                    for c in range(gn):
                        lhs = hmT[:, (par * cfg.G3 + c) * T + ti * 128:
                                  (par * cfg.G3 + c) * T + (ti + 1) * 128]
                        for si, sub in enumerate(range(0, hw, 512)):
                            sw = min(512, hw - sub)
                            last = (c == gn - 1 and si == nsub - 1)
                            TE.run("matmul", pp[:, sub:sub + sw], lhs,
                                   wdr[:, (par * cfg.G3 + c) * HPW + sub:
                                       (par * cfg.G3 + c) * HPW + sub + sw],
                                   start=(c == 0), stop=(c == gn - 1),
                                   sem=(S.mm1 if last else None))
                    mm3_hist[(hpi, gi, ti)] = S.mm1.v
                    VE.wait(S.mm1, mm3_hist[(hpi, gi, ti)])
                    od = outs[:, ti * HPW: ti * HPW + hw]
                    if gi == 0:
                        if hpi > 0:
                            VE.wait(S.fl, fl_hist[(hpi - 1, NGRP - 1, ti)])
                            VE.wait(S.outp)
                        VE.run("tensor_copy", od, pp[:, 0:hw], sem=S.fl)
                    else:
                        VE.wait(S.fl, fl_hist[(hpi, gi - 1, ti)])
                        VE.run("tensor_tensor", out=od, in0=od, in1=pp[:, 0:hw],
                               op=aop.add, sem=S.fl)
                    fl_hist[(hpi, gi, ti)] = S.fl.v
                    if gi == NGRP - 1:
                        sl = ti % 2
                        xs = xg[:, sl * HPW: sl * HPW + hw]
                        gs_ = xg[:, (2 + sl) * HPW: (2 + sl) * HPW + hw]
                        if (hpi, ti - 2) in xg_hist:
                            SY.wait(S.fl, xg_hist[(hpi, ti - 2)])
                        elif hpi > 0 and (hpi - 1, ti + TT - 2) in xg_hist:
                            SY.wait(S.fl, xg_hist[(hpi - 1, ti + TT - 2)])
                        SY.dma(xgld2[sl], xs,
                               x_d[ti * 128:(ti + 1) * 128, hoff:hoff + hw])
                        SY.dma(xgld2[sl], gs_,
                               g_d[ti * 128:(ti + 1) * 128, hoff:hoff + hw])
                        VE.wait(xgld2[sl])
                        VE.wait(S.fl)
                        VE.run("tensor_tensor", out=od, in0=od, in1=gs_,
                               op=aop.mult, sem=S.fl)
                        VE.wait(S.fl)
                        VE.run("tensor_tensor", out=od, in0=od, in1=xs,
                               op=aop.add, sem=S.fl)
                        xg_hist[(hpi, ti)] = S.fl.v
                        fl_hist[(hpi, gi, ti)] = S.fl.v
                        SY.wait(S.fl, S.fl.v)
                        SY.dma(S.outp, out_d[ti * 128:(ti + 1) * 128, hoff:hoff + hw],
                               od)
        SY.wait(S.hmtt)
        SY.wait(S.outp, S.outp.v)
        qs["gpsimd"].wait(S.outp, S.outp.v)
    flush_block(nc, "p3b", qs)

    st.close()
    return nc


def _prev_psum_use(key, ngrp, tt):
    hpi, gi, ti = key
    ti2 = ti - 4
    if ti2 >= 0:
        return (hpi, gi, ti2)
    gi2 = gi - 1
    if gi2 >= 0:
        return (hpi, gi2, tt + ti2)
    if hpi > 0:
        return (hpi - 1, ngrp - 1, tt + ti2)
    return None


def _host_pack(cfg, w_gate, w_up, w_down, packed, scales):
    H, I, O, KC, BLK, IC = cfg.H, cfg.I, cfg.O, cfg.KC, cfg.BLK, cfg.IC
    blockperm = np.array([2 * (p % 64) + (p // 64) for p in range(128)])
    sig = (np.arange(H) // 128) * 128 + blockperm[np.arange(H) % 128]

    def pack_w(w):
        wt = np.ascontiguousarray(w.T[sig])
        pieces = []
        for (poff, W) in cfg.panels1:
            blk = wt[:, poff:poff + W].reshape(KC, 128, W)
            for g in range(cfg.NGR):
                ck = min(cfg.GK, KC - g * cfg.GK)
                pieces.append(np.ascontiguousarray(blk[g * cfg.GK:g * cfg.GK + ck]).ravel())
        return np.concatenate(pieces)

    wg_flat = pack_w(np.asarray(w_gate, np.float32))
    wu_flat = pack_w(np.asarray(w_up, np.float32))
    wdt = np.ascontiguousarray(np.asarray(w_down, np.float32).T).reshape(IC, 128, O)
    pkt = np.ascontiguousarray(np.asarray(packed, np.int32).transpose(1, 2, 0))
    sc = np.asarray(scales, np.float32)[:, :, 0].T
    scb = np.ascontiguousarray(np.broadcast_to(sc[:, None, :], (BLK, 128, O)))
    return wg_flat, wu_flat, wdt, pkt, scb


def kernel(L0_attn_x, L0_attn_w_gate, L0_attn_w_up, L0_attn_w_down,
           L0_ffn_packed_weight, L0_ffn_scales):
    cfg = Cfg()
    x = np.asarray(L0_attn_x, np.float32)
    B = x.shape[0]
    wg_flat, wu_flat, wdt, pkt, scb = _host_pack(
        cfg, L0_attn_w_gate, L0_attn_w_up, L0_attn_w_down,
        L0_ffn_packed_weight, L0_ffn_scales)
    nc = build_nc(cfg)
    in_maps = []
    for c in range(B):
        in_maps.append(dict(x=np.ascontiguousarray(x[c]), wg=wg_flat, wu=wu_flat,
                            wdt=wdt, pkt=pkt, scb=scb))
    res = run_bass_kernel_spmd(nc, in_maps, core_ids=list(range(B)))
    out = np.stack([np.asarray(r["out"], np.float32) for r in res.results])
    return out


# version 11
# speedup vs baseline: 1.1225x; 1.0692x over previous
import contextlib
import numpy as np
import ml_dtypes
import concourse.bass as bass
import concourse.mybir as mybir
from concourse.bass_utils import run_bass_kernel_spmd
from concourse.alu_op_type import AluOpType as aop

F32 = mybir.dt.float32
F16 = mybir.dt.float16
BF16 = mybir.dt.bfloat16
I32 = mybir.dt.int32
F8 = mybir.dt.float8e4
AF = mybir.ActivationFunctionType
AX = mybir.AxisListType
bfnp = ml_dtypes.bfloat16


class Cfg:
    def __init__(self, T=2048, H=4096, I=11008, O=4096, NIT=14, n_cores=8):
        self.T, self.H, self.I, self.O = T, H, I, O
        self.NIT = NIT
        self.n_cores = n_cores
        self.TT = T // 128
        self.KC = H // 128
        self.IC = I // 128
        self.BLK = H // 128
        self.K = I // 2
        self.W1 = 384
        ws = []
        off = 0
        while off < I:
            w = min(self.W1, I - off)
            ws.append((off, w))
            off += w
        self.panels1 = ws
        self.GK = 4
        self.NGR = (self.KC + self.GK - 1) // self.GK
        self.TG = 2
        assert self.TT % self.TG == 0
        assert self.TT <= 16
        self.W2 = 512 if O >= 512 else O
        assert O % self.W2 == 0
        self.OP = O // self.W2
        self.TH2 = (self.TT + 7) // 8
        self.BB = 2
        assert self.BLK % self.BB == 0
        self.G3 = 8
        gs = []
        ic = 0
        while ic < self.IC:
            g = min(self.G3, self.IC - ic)
            gs.append((ic, g))
            ic += g
        self.groups3 = gs
        self.HP = 1024 if O >= 1024 else O
        hp = []
        off = 0
        while off < O:
            w = min(self.HP, O - off)
            hp.append((off, w))
            off += w
        self.hpanels = hp
        self.TH0 = 8.0
        self.W0 = 4.0


class Sem:
    def __init__(self, st, nc, name):
        self.h = st.enter_context(nc.semaphore(name))
        self.v = 0


class Q:

    def __init__(self, name):
        self.name = name
        self.ops = []
        self._lastwait = {}

    def add(self, f):
        self.ops.append(f)

    def wait(self, sem, val=None):
        v = sem.v if val is None else val
        if v <= 0:
            return
        if self._lastwait.get(id(sem), -1) >= v:
            return
        self._lastwait[id(sem)] = v
        self.ops.append(lambda e, h=sem.h, v=v: e.wait_ge(h, v))

    @staticmethod
    def _ap(x):
        if isinstance(x, bass.AP) or not hasattr(x, "ap"):
            return x
        return x.ap()

    def dma(self, sem, out, in_, **kw):
        sem.v += 16
        out, in_ = self._ap(out), self._ap(in_)
        def f(e, out=out, in_=in_, kw=kw, h=sem.h):
            e.dma_start(out=out, in_=in_, **kw).then_inc(h, 16)
        self.ops.append(f)

    def run(self, fname, *args, sem=None, inc=1, **kw):
        if sem is not None:
            sem.v += inc
        args = tuple(self._ap(a) for a in args)
        kw = {k: (self._ap(v) if k in ("out", "in_", "in0", "in1") else v)
              for k, v in kw.items()}
        def f(e, fname=fname, args=args, kw=kw, sem=sem, inc=inc):
            r = getattr(e, fname)(*args, **kw)
            if sem is not None:
                r.then_inc(sem.h, inc)
        self.ops.append(f)


def flush_block(nc, name, qs):
    with nc.Block(name) as blk:
        for q in qs.values():
            if q.ops:
                runner = getattr(blk, q.name)
                def body(eng, ops=q.ops):
                    for f in ops:
                        f(eng)
                runner(body)


def new_queues():
    return {n: Q(n) for n in ("sync", "scalar", "vector", "gpsimd", "tensor")}


def build_nc(cfg: Cfg):
    T, H, I, O = cfg.T, cfg.H, cfg.I, cfg.O
    TT, KC, IC, BLK = cfg.TT, cfg.KC, cfg.IC, cfg.BLK
    nc = bass.Bass("TRN2", target_bir_lowering=False)
    st = contextlib.ExitStack()

    x_d = nc.dram_tensor("x", [T, H], F32, kind="ExternalInput").ap()
    ng = 4 * 128
    wg_sz = H * I
    wg_d = nc.dram_tensor("wg", [wg_sz], F32, kind="ExternalInput").ap()
    wu_d = nc.dram_tensor("wu", [wg_sz], F32, kind="ExternalInput").ap()
    wdt_d = nc.dram_tensor("wdt", [IC, 128, O], F32, kind="ExternalInput").ap()
    pkt_d = nc.dram_tensor("pkt", [BLK, 64, O], I32, kind="ExternalInput").ap()
    scb_d = nc.dram_tensor("scb", [BLK, 128, O], F32, kind="ExternalInput").ap()
    out_d = nc.dram_tensor("out", [T, O], F32, kind="ExternalOutput").ap()
    xn_d = nc.dram_tensor("xn_s", [T, H], BF16, kind="Internal").ap()
    s_d = nc.dram_tensor("s_s", [TT, 128, I], F16, kind="Internal").ap()
    sup_d = nc.dram_tensor("sup_s", [TT, 128, I], BF16, kind="Internal").ap()
    mask_d = nc.dram_tensor("mask_s", [TT, 128, I], F8, kind="Internal").ap()
    hm_d = nc.dram_tensor("hm_s", [T, I], BF16, kind="Internal").ap()
    g_d = nc.dram_tensor("g_s", [T, O], F32, kind="Internal").ap()
    wq_dram = nc.dram_tensor("wq_s", [BLK, 128, O], BF16, kind="Internal").ap()

    gran_off = {}
    off = 0
    for p, (poff, W) in enumerate(cfg.panels1):
        for g in range(cfg.NGR):
            ck = min(cfg.GK, KC - g * cfg.GK)
            gran_off[(p, g)] = (off, ck, W)
            off += ck * 128 * W
    assert off == wg_sz

    sems = {}
    for name in ("xld", "v0", "a0", "spxn", "trx",
                 "pegr", "mm1", "e1a", "e1v", "sp1",
                 "pkld", "scld", "dq", "evg", "gsp", "peb",
                 "sld", "chA", "chD", "mk", "msp",
                 "hmld", "hmtt", "hmsp", "trh", "wdld", "xgld", "fl", "outp",
                 "wqsp"):
        sems[name] = Sem(st, nc, name)
    S = type("S", (), sems)
    wld = [Sem(st, nc, f"wld{g}") for g in range(cfg.NGR)]
    xld2 = [Sem(st, nc, f"xld{i}") for i in range(2)]
    pkld2 = [Sem(st, nc, f"pkld{i}") for i in range(2)]
    scld2 = [Sem(st, nc, f"scld{i}") for i in range(2)]
    sld2 = {c: Sem(st, nc, f"sld{c}") for c in "AD"}
    msp2 = {c: Sem(st, nc, f"msp{c}") for c in "AD"}
    hmld2 = [Sem(st, nc, f"hmld{i}") for i in range(2)]
    trh2 = [Sem(st, nc, f"trh{i}") for i in range(2)]
    wdld2 = [Sem(st, nc, f"wdld{i}") for i in range(2)]
    xgld2 = [Sem(st, nc, f"xgld{i}") for i in range(2)]
    wqld2 = [Sem(st, nc, f"wqld{i}") for i in range(2)]
    gsp2 = [Sem(st, nc, f"gsp{i}") for i in range(2)]
    asp2 = {c: Sem(st, nc, f"asp{c}") for c in "AD"}

    stats = st.enter_context(nc.sbuf_tensor("stats", [128, 16], F32))
    th_a = st.enter_context(nc.sbuf_tensor("th_a", [128, TT], F32))
    th_b = st.enter_context(nc.sbuf_tensor("th_b", [128, TT], F32))
    cnt_t = st.enter_context(nc.sbuf_tensor("cnt_t", [128, TT], F32))
    sgn_t = st.enter_context(nc.sbuf_tensor("sgn_t", [128, TT], F32))
    xnT_ctx = nc.sbuf_tensor("xnT", [128, KC * T], BF16)
    xnT = xnT_ctx.__enter__()

    qs = new_queues()
    with (nc.sbuf_tensor("xbuf", [128, 2 * H], F32) as xbuf,
          nc.sbuf_tensor("xnst", [128, 2 * H], BF16) as xnst,
          nc.sbuf_tensor("junk0", [128, H], F8) as junk8):
        SY, SC, VE = qs["sync"], qs["scalar"], qs["vector"]
        rel_x = {}
        for tt in range(TT):
            par = tt % 2
            xb = xbuf[:, par * H:(par + 1) * H]
            xs = xnst[:, par * H:(par + 1) * H]
            if tt - 2 >= 0:
                for sem, val in rel_x[tt - 2]:
                    SY.wait(sem, val)
            SY.dma(xld2[par], xb, x_d[tt * 128:(tt + 1) * 128, :])
            c = (tt % 2) * 8
            VE.wait(xld2[par])
            VE.run("reduce_sum", stats[:, c:c + 1], xb, axis=AX.X, sem=S.v0)
            VE.wait(S.v0)
            VE.run("tensor_scalar", out=stats[:, c + 1:c + 2], in0=stats[:, c:c + 1],
                   scalar1=-1.0 / H, scalar2=None, op0=aop.mult, sem=S.v0)
            SC.wait(S.v0)
            SC.wait(xld2[par], xld2[par].v)
            SC.run("activation", out=junk8[:, :H], in_=xb, func=AF.Square,
                   bias=stats[:, c + 1:c + 2], scale=1.0,
                   accum_out=stats[:, c + 2:c + 3], sem=S.a0)
            VE.wait(S.a0)
            VE.run("tensor_scalar", out=stats[:, c + 3:c + 4], in0=stats[:, c + 2:c + 3],
                   scalar1=1.0 / H, scalar2=1e-5, op0=aop.mult, op1=aop.add, sem=S.v0)
            SC.wait(S.v0)
            SC.run("activation", out=stats[:, c + 4:c + 5], in_=stats[:, c + 3:c + 4],
                   func=AF.Sqrt, sem=S.a0)
            VE.wait(S.a0)
            VE.run("reciprocal", stats[:, c + 5:c + 6], stats[:, c + 4:c + 5], sem=S.v0)
            VE.wait(S.v0)
            if tt - 2 >= 0:
                VE.wait(S.spxn)
            xs_perm = xs.rearrange("p (b r c) -> p b c r", b=KC, r=2)
            VE.run("tensor_scalar", out=xs_perm, in0=xb,
                   scalar1=stats[:, c + 1:c + 2], scalar2=stats[:, c + 5:c + 6],
                   op0=aop.add, op1=aop.mult, sem=S.v0)
            rel_x[tt] = [(S.a0, S.a0.v), (S.v0, S.v0.v)]
            SY.wait(S.v0)
            SY.dma(S.spxn, xn_d[tt * 128:(tt + 1) * 128, :], xs)
        SCQ = qs["scalar"]
        SCQ.wait(S.spxn)
        for k in range(KC):
            SCQ.dma(S.trx, xnT[:, k * T:(k + 1) * T],
                    xn_d[:, k * 128:(k + 1) * 128], transpose=True)
    flush_block(nc, "p0", qs)

    qs = new_queues()
    slotW = cfg.GK * cfg.W1
    with (nc.sbuf_tensor("wgr", [128, cfg.NGR * slotW], BF16) as wgr,
          nc.sbuf_tensor("wur", [128, cfg.NGR * slotW], BF16) as wur,
          nc.sbuf_tensor("sst", [128, 2 * cfg.TG * cfg.W1], F16) as sst,
          nc.sbuf_tensor("supst", [128, 2 * cfg.TG * cfg.W1], BF16) as supst,
          nc.sbuf_tensor("pkq", [128, 2 * 1024], I32) as pkq,
          nc.sbuf_tensor("nibq", [128, 1024], I32) as nibq,
          nc.sbuf_tensor("scq", [128, 2 * 1024], F32) as scq,
          nc.sbuf_tensor("wqst", [128, 2 * 1024], BF16) as wqst,
          contextlib.ExitStack() as pst):
        psum = [pst.enter_context(nc.psum_tensor(f"p1_{i}", [128, 512], F32))
                for i in range(8)]
        SY, SC, VE, GP, TE = (qs[n] for n in ("sync", "scalar", "vector", "gpsimd", "tensor"))
        TE.wait(S.trx)
        NGROUP = TT // cfg.TG
        npan = len(cfg.panels1)
        ev_hist = {}
        sp_hist = {}
        gq_hist = {}
        def load_gran(p, g):
            offs, ck, W = gran_off[(p, g)]
            if p > 0:
                if g < cfg.NGR - 1:
                    GP.wait(S.pegr, (p - 1) * (cfg.NGR - 1) + g + 1)
                else:
                    GP.wait(S.mm1, p * NGROUP)
            for (ring, src) in ((wgr, wg_d), (wur, wu_d)):
                dst = ring[:, g * slotW + 0: g * slotW + ck * W]
                dst = dst.rearrange("p (c w) -> p c w", c=ck)
                srcap = src[offs: offs + ck * 128 * W]
                srcap = srcap.rearrange("(c p w) -> p c w", c=ck, p=128)
                GP.dma(wld[g], dst, srcap)
        for g in range(cfg.NGR):
            load_gran(0, g)
        DCW = min(512, O)
        batches = [(b0, c0) for b0 in range(0, BLK, 2) for c0 in range(0, O, DCW)]
        NBAT = len(batches)
        dqb_hist = {}
        deq_i = [0]

        def deq_step():
            bi = deq_i[0]
            if bi >= NBAT:
                return
            deq_i[0] += 1
            b0, c0 = batches[bi]
            par = bi % 2
            bw2 = 2 * DCW
            if bi - 2 >= 0:
                SY.wait(S.dq, dqb_hist[bi - 2])
            src = pkt_d[b0:b0 + 2, :, c0:c0 + DCW].rearrange("b p w -> p b w")
            SY.dma(pkld2[par], pkq[0:64, par * bw2:(par + 1) * bw2].rearrange(
                "p (b w) -> p b w", b=2), src)
            SY.dma(pkld2[par], pkq[64:128, par * bw2:(par + 1) * bw2].rearrange(
                "p (b w) -> p b w", b=2), src)
            scsrc = scb_d[b0:b0 + 2, :, c0:c0 + DCW].rearrange("b p w -> p b w")
            SY.dma(scld2[par], scq[:, par * bw2:(par + 1) * bw2].rearrange(
                "p (b w) -> p b w", b=2), scsrc)
            VE.wait(pkld2[par])
            VE.wait(S.dq)
            VE.wait(S.wqsp)
            VE.run("tensor_scalar", out=nibq[0:64, :bw2],
                   in0=pkq[0:64, par * bw2:(par + 1) * bw2], scalar1=4,
                   scalar2=None, op0=aop.logical_shift_right, sem=S.dq)
            VE.run("tensor_scalar", out=nibq[64:128, :bw2],
                   in0=pkq[64:128, par * bw2:(par + 1) * bw2], scalar1=15,
                   scalar2=None, op0=aop.bitwise_and, sem=S.dq)
            VE.wait(scld2[par])
            VE.wait(S.dq)
            VE.run("scalar_tensor_tensor", out=wqst[:, par * bw2:(par + 1) * bw2],
                   in0=nibq[:, :bw2], scalar=8.0,
                   in1=scq[:, par * bw2:(par + 1) * bw2], op0=aop.subtract,
                   op1=aop.mult, sem=S.dq)
            dqb_hist[bi] = S.dq.v
            GP.wait(S.dq, dqb_hist[bi])
            GP.dma(S.wqsp, wq_dram[b0, :, c0:c0 + DCW],
                   wqst[:, par * bw2: par * bw2 + DCW])
            GP.dma(S.wqsp, wq_dram[b0 + 1, :, c0:c0 + DCW],
                   wqst[:, par * bw2 + DCW:(par + 1) * bw2])

        gidx = 0
        ngroups_total = len(cfg.panels1) * NGROUP
        for p, (poff, W) in enumerate(cfg.panels1):
            for tg in range(NGROUP):
                par = tg % 2
                tiles = (cfg.TG * tg, cfg.TG * tg + 1)
                if gidx - 2 >= 0:
                    ea, ev = ev_hist[gidx - 2]
                    TE.wait(S.e1a, ea)
                    TE.wait(S.e1v, ev)
                for k in range(KC):
                    g = k // cfg.GK
                    if tg == 0:
                        TE.wait(wld[g], 32 * (p + 1))
                    ck_local = k % cfg.GK
                    woff = g * slotW + ck_local * W
                    for tl, ti in enumerate(tiles):
                        lhs = xnT[:, k * T + ti * 128: k * T + (ti + 1) * 128]
                        pg = psum[par * 4 + 2 * tl][:, 0:W]
                        pu = psum[par * 4 + 2 * tl + 1][:, 0:W]
                        kw = dict(start=(k == 0), stop=(k == KC - 1))
                        last_of_group = (k == KC - 1 and tl == len(tiles) - 1)
                        rel_gran = (tg == NGROUP - 1 and ck_local == 0 and g > 0
                                    and tl == 0)
                        if rel_gran:
                            TE.run("matmul", pg, lhs, wgr[:, woff:woff + W],
                                   sem=S.pegr, **kw)
                            if p + 1 < npan:
                                load_gran(p + 1, g - 1)
                        else:
                            TE.run("matmul", pg, lhs, wgr[:, woff:woff + W], **kw)
                        if last_of_group:
                            TE.run("matmul", pu, lhs, wur[:, woff:woff + W],
                                   sem=S.mm1, **kw)
                        else:
                            TE.run("matmul", pu, lhs, wur[:, woff:woff + W], **kw)
                gq_hist[gidx] = S.mm1.v
                if tg == NGROUP - 1 and p + 1 < npan:
                    load_gran(p + 1, cfg.NGR - 1)
                SC.wait(S.mm1, gq_hist[gidx])
                if gidx - 2 >= 0:
                    SC.wait(S.sp1)
                for tl, ti in enumerate(tiles):
                    sslice = sst[:, (par * cfg.TG + tl) * cfg.W1:
                                 (par * cfg.TG + tl) * cfg.W1 + W]
                    SC.run("activation", out=sslice, in_=psum[par * 4 + 2 * tl][:, 0:W],
                           func=AF.Silu, sem=S.e1a)
                ev_a = S.e1a.v
                VE.wait(S.e1a, ev_a)
                if gidx - 2 >= 0:
                    VE.wait(S.sp1)
                for tl, ti in enumerate(tiles):
                    sslice = sst[:, (par * cfg.TG + tl) * cfg.W1:
                                 (par * cfg.TG + tl) * cfg.W1 + W]
                    supslice = supst[:, (par * cfg.TG + tl) * cfg.W1:
                                     (par * cfg.TG + tl) * cfg.W1 + W]
                    VE.run("tensor_tensor", out=supslice, in0=sslice,
                           in1=psum[par * 4 + 2 * tl + 1][:, 0:W], op=aop.mult,
                           sem=S.e1v)
                ev_v = S.e1v.v
                ev_hist[gidx] = (ev_a, ev_v)
                SY.wait(S.e1a, ev_a)
                SY.wait(S.e1v, ev_v)
                for tl, ti in enumerate(tiles):
                    sslice = sst[:, (par * cfg.TG + tl) * cfg.W1:
                                 (par * cfg.TG + tl) * cfg.W1 + W]
                    supslice = supst[:, (par * cfg.TG + tl) * cfg.W1:
                                     (par * cfg.TG + tl) * cfg.W1 + W]
                    SY.dma(S.sp1, s_d[ti, :, poff:poff + W], sslice)
                    SY.dma(S.sp1, sup_d[ti, :, poff:poff + W], supslice)
                sp_hist[gidx] = S.sp1.v
                gidx += 1
                while deq_i[0] * ngroups_total < NBAT * gidx:
                    deq_step()
        while deq_i[0] < NBAT:
            deq_step()
    flush_block(nc, "p1", qs)

    qs = new_queues()
    IH = (I + 1) // 2
    QH = min(1024, I)
    W2 = cfg.W2
    with (nc.sbuf_tensor("abufA", [128, I], F16) as abufA,
          nc.sbuf_tensor("abufD", [128, I], F16) as abufD,
          nc.sbuf_tensor("junkA", [128, IH], F8) as junkA,
          nc.sbuf_tensor("junkD", [128, IH], F8) as junkD,
          nc.sbuf_tensor("supq", [128, 2 * QH], BF16) as supq,
          nc.sbuf_tensor("asq", [128, 2 * QH], F16) as asq,
          nc.sbuf_tensor("wqr", [128, 2 * 2 * W2], BF16) as wqr,
          nc.sbuf_tensor("gst", [128, 2 * W2], F32) as gst,
          contextlib.ExitStack() as pst):
        psum = [pst.enter_context(nc.psum_tensor(f"p2_{i}", [128, 512], F32))
                for i in range(8)]
        SY, SC, VE, GP, TE = (qs[n] for n in ("sync", "scalar", "vector", "gpsimd", "tensor"))
        for q in (SY, SC, VE, GP):
            q.wait(S.sp1)
            q.wait(S.wqsp)
        TE.wait(S.trx)

        US_MM = 0.00022 * 512
        tPE = 0.0
        evac_pts = []
        peb_hist = {}
        NHALF = cfg.TH2
        halves = [list(range(h * 8, min(TT, h * 8 + 8))) for h in range(NHALF)]
        nsilu_per = [len(h) for h in halves]
        evg_after = {}
        cum = 0
        for op_ in range(cfg.OP):
            for th_ in range(NHALF):
                cum += nsilu_per[th_]
                evg_after[(op_, th_)] = cum
        for op_ in range(cfg.OP):
            ocols = slice(op_ * W2, (op_ + 1) * W2)
            for th_ in range(NHALF):
                tiles = halves[th_]
                for b0 in range(0, BLK, 2):
                    gpar = (b0 // 2) % 2
                    seqg = (op_ * NHALF + th_) * (BLK // 2) + b0 // 2
                    if seqg - 2 >= 0:
                        GP.wait(S.peb, peb_hist[seqg - 2])
                    GP.dma(wqld2[gpar],
                           wqr[:, gpar * 2 * W2:(gpar + 1) * 2 * W2].rearrange(
                               "p (b w) -> p b w", b=2),
                           wq_dram[b0:b0 + 2, :, ocols].rearrange("b p w -> p b w"))
                    TE.wait(wqld2[gpar])
                    if op_ > 0 and b0 == 0 and th_ == 0:
                        TE.wait(S.evg, evg_after[(op_ - 1, NHALF - 1)])
                    if th_ == 1 and b0 == 0:
                        TE.wait(S.evg, evg_after[(op_, 0)])
                    for bl in range(2):
                        b = b0 + bl
                        rhs = wqr[:, (gpar * 2 + bl) * W2:(gpar * 2 + bl + 1) * W2]
                        for n_, ti in enumerate(tiles):
                            lhs = xnT[:, b * T + ti * 128: b * T + (ti + 1) * 128]
                            pq = psum[n_][:, 0:W2]
                            TE.run("matmul", pq, lhs, rhs,
                                   start=(b == 0), stop=(b == BLK - 1),
                                   sem=(S.peb if (bl == 1 and n_ == len(tiles) - 1)
                                        else None))
                        tPE += len(tiles) * US_MM
                    peb_hist[seqg] = S.peb.v
                evac_pts.append((S.peb.v, op_, th_, tiles, tPE))

        alltiles = list(range(TT))
        NA = min((9 * TT + 15) // 16, TT)
        tiles_A = alltiles[:NA]
        tiles_D = alltiles[NA:]
        Kh = cfg.K + 0.5
        VE.run("memset", th_a[:, :], cfg.TH0, sem=S.chD)
        th_bufs = [th_a, th_b]
        chA_after = {}
        opsA = 1 + 5 * cfg.NIT
        for n in range(len(tiles_A)):
            chA_after[n] = (n + 1) * opsA
        sldA_hist = {}
        aspA_hist = {}
        aspD_hist = {}
        chD_after = {}
        hq_seq = [0]
        hq_tt = {}

        def hmid_tile(j, chs, chv, asps, aspv):
            fin = th_bufs[cfg.NIT % 2]
            for c0 in range(0, I, QH):
                cw = min(QH, I - c0)
                seq = hq_seq[0]
                par = seq % 2
                sq = supq[:, par * QH: par * QH + cw]
                aq = asq[:, par * QH: par * QH + cw]
                if seq - 2 >= 0:
                    SY.wait(S.mk, hq_tt[seq - 2])
                    SY.wait(S.hmsp)
                SY.wait(asps, aspv)
                SY.dma(hmld2[par], sq, sup_d[j, :, c0:c0 + cw])
                SY.dma(hmld2[par], aq, s_d[j, :, c0:c0 + cw])
                VE.wait(hmld2[par])
                VE.wait(chs, chv)
                if seq - 2 >= 0:
                    VE.wait(S.hmsp)
                VE.run("scalar_tensor_tensor", out=sq, in0=aq,
                       scalar=fin[:, j:j + 1], in1=sq, op0=aop.is_gt,
                       op1=aop.mult, sem=S.mk)
                hq_tt[seq] = S.mk.v
                SY.wait(S.mk, hq_tt[seq])
                SY.dma(S.hmsp, hm_d[j * 128:(j + 1) * 128, c0:c0 + cw], sq)
                hq_seq[0] += 1

        def bisect_D(j, abuf, jbuf):
            VE.wait(sld2["D"])
            VE.run("scalar_tensor_tensor", out=abuf, in0=abuf, scalar=-1.0,
                   in1=abuf, op0=aop.mult, op1=aop.max, sem=S.chD)
            w = cfg.W0
            for it in range(cfg.NIT):
                cur = th_bufs[it % 2]
                nxt = th_bufs[(it + 1) % 2]
                VE.wait(S.chD)
                VE.run("tensor_scalar", out=jbuf, in0=abuf[:, 0:IH],
                       scalar1=cur[:, j:j + 1], scalar2=None, op0=aop.is_gt,
                       op1=aop.add, accum_out=cnt_t[:, j:j + 1], sem=S.chD)
                VE.wait(S.chD)
                VE.run("tensor_scalar", out=jbuf[:, 0:I - IH], in0=abuf[:, IH:I],
                       scalar1=cur[:, j:j + 1], scalar2=None, op0=aop.is_gt,
                       op1=aop.add, accum_out=sgn_t[:, j:j + 1], sem=S.chD)
                VE.wait(S.chD)
                VE.run("tensor_tensor", out=cnt_t[:, j:j + 1],
                       in0=cnt_t[:, j:j + 1], in1=sgn_t[:, j:j + 1],
                       op=aop.add, sem=S.chD)
                VE.wait(S.chD)
                VE.run("tensor_scalar", out=sgn_t[:, j:j + 1],
                       in0=cnt_t[:, j:j + 1], scalar1=float(Kh), scalar2=None,
                       op0=aop.is_gt, sem=S.chD)
                VE.wait(S.chD)
                VE.run("tensor_scalar", out=sgn_t[:, j:j + 1],
                       in0=sgn_t[:, j:j + 1], scalar1=float(2 * w),
                       scalar2=float(w), op0=aop.mult, op1=aop.subtract,
                       sem=S.chD)
                VE.wait(S.chD)
                VE.run("tensor_tensor", out=nxt[:, j:j + 1],
                       in0=cur[:, j:j + 1], in1=sgn_t[:, j:j + 1],
                       op=aop.add, sem=S.chD)
                w *= 0.5

        NT = max(len(tiles_A), len(tiles_D))
        for n in range(NT):
            if n < len(tiles_A):
                j = tiles_A[n]
                if n - 1 >= 0:
                    SY.wait(asp2["A"])
                    SY.wait(S.chA, chA_after[n - 1])
                SY.dma(sld2["A"], abufA, s_d[j, :, :])
                sldA_hist[n] = sld2["A"].v
                SY.wait(S.chA, n * opsA + 1)
                SY.dma(asp2["A"], s_d[j, :, :], abufA)
                aspA_hist[n] = asp2["A"].v
            if n < len(tiles_D):
                j = tiles_D[n]
                if n - 1 >= 0:
                    SY.wait(asp2["D"])
                    SY.wait(S.chD, S.chD.v)
                SY.dma(sld2["D"], abufD, s_d[j, :, :])
                bisect_D(j, abufD, junkD)
                SY.wait(S.chD, S.chD.v - 6 * cfg.NIT)
                SY.dma(asp2["D"], s_d[j, :, :], abufD)
                chD_after[n] = S.chD.v
                aspD_hist[n] = asp2["D"].v
            if n - 1 >= 0:
                if n - 1 < len(tiles_A):
                    hmid_tile(tiles_A[n - 1], S.chA, chA_after[n - 1],
                              asp2["A"], aspA_hist[n - 1])
                if n - 1 < len(tiles_D):
                    hmid_tile(tiles_D[n - 1], S.chD, chD_after[n - 1],
                              asp2["D"], aspD_hist[n - 1])
            if n == max(len(tiles_A), len(tiles_D)) - 1:
                m = n
                if m < len(tiles_A):
                    hmid_tile(tiles_A[m], S.chA, chA_after[m],
                              asp2["A"], aspA_hist[m])
                if m < len(tiles_D):
                    hmid_tile(tiles_D[m], S.chD, chD_after[m],
                              asp2["D"], aspD_hist[m])

        a_steps = []
        for n, j in enumerate(tiles_A):

            def start_step(n=n, j=j):
                SC.wait(sld2["A"], sldA_hist[n])
                SC.wait(S.chD, 1)
                SC.run("activation", out=abufA, in_=abufA, func=AF.Abs, sem=S.chA)
            a_steps.append((10.0, start_step))
            w = cfg.W0
            for it in range(cfg.NIT):
                cur = th_bufs[it % 2]
                nxt = th_bufs[(it + 1) % 2]

                def iter_step(j=j, cur=cur, nxt=nxt, w=w):
                    SC.wait(S.chA)
                    SC.run("activation", out=junkA, in_=abufA[:, 0:IH],
                           func=AF.Sign, bias=cur[:, j:j + 1], scale=-1.0,
                           accum_out=cnt_t[:, j:j + 1], sem=S.chA)
                    SC.wait(S.chA)
                    SC.run("activation", out=junkA[:, 0:I - IH],
                           in_=abufA[:, IH:I], func=AF.Sign,
                           bias=cur[:, j:j + 1], scale=-1.0,
                           accum_out=sgn_t[:, j:j + 1], sem=S.chA)
                    SC.wait(S.chA)
                    SC.run("activation", out=cnt_t[:, j:j + 1],
                           in_=cnt_t[:, j:j + 1], func=AF.Identity, scale=1.0,
                           bias=sgn_t[:, j:j + 1], sem=S.chA)
                    SC.wait(S.chA)
                    SC.run("activation", out=sgn_t[:, j:j + 1],
                           in_=cnt_t[:, j:j + 1], func=AF.Sign, scale=-1.0,
                           bias=0.0, sem=S.chA)
                    SC.wait(S.chA)
                    SC.run("activation", out=nxt[:, j:j + 1],
                           in_=sgn_t[:, j:j + 1], func=AF.Identity, scale=float(w),
                           bias=cur[:, j:j + 1], sem=S.chA)
                a_steps.append((10.8, iter_step))
                w *= 0.5

        fin_th = th_bufs[cfg.NIT % 2]
        ai = 0
        tACT = 0.0
        gs_seq = [0]
        for (peb_tgt, op_, th_, tiles, t_ready) in evac_pts:
            while ai < len(a_steps) and tACT + a_steps[ai][0] <= t_ready:
                tACT += a_steps[ai][0]
                a_steps[ai][1]()
                ai += 1
            SC.wait(S.peb, peb_tgt)
            ocols = slice(op_ * W2, (op_ + 1) * W2)
            for n_, ti in enumerate(tiles):
                sl = gs_seq[0] % 2
                if gs_seq[0] - 2 >= 0:
                    SC.wait(gsp2[sl])
                SC.run("activation", out=gst[:, sl * W2:(sl + 1) * W2],
                       in_=psum[n_][:, 0:W2], func=AF.Silu, sem=S.evg)
                SC.wait(S.evg, S.evg.v)
                SC.dma(gsp2[sl], g_d[ti * 128:(ti + 1) * 128, ocols],
                       gst[:, sl * W2:(sl + 1) * W2])
                gs_seq[0] += 1
            tACT = max(tACT, t_ready) + len(tiles) * 1.5
        while ai < len(a_steps):
            a_steps[ai][1]()
            ai += 1
        chain_done = (S.chA.v, S.chD.v)
    flush_block(nc, "p2p3a", qs)

    xnT_ctx.__exit__(None, None, None)

    qs = new_queues()
    NGRP = len(cfg.groups3)
    NHP = len(cfg.hpanels)
    HPW = cfg.HP
    with (nc.sbuf_tensor("hmT", [128, 2 * cfg.G3 * T], BF16) as hmT,
          nc.sbuf_tensor("wdr", [128, 2 * cfg.G3 * HPW], BF16) as wdr,
          nc.sbuf_tensor("outs", [128, TT * HPW], F32) as outs,
          nc.sbuf_tensor("xg", [128, 4 * HPW], F32) as xg,
          contextlib.ExitStack() as pst):
        psum = [pst.enter_context(nc.psum_tensor(f"p3_{i}", [128, 1024], F32))
                for i in range(4)]
        SY, SC, VE, GP, TE = (qs[n] for n in ("sync", "scalar", "vector", "gpsimd", "tensor"))
        SCQ = qs["scalar"]
        for q in (SY, SCQ, GP, VE):
            q.wait(S.sp1)
            q.wait(gsp2[0])
            q.wait(gsp2[1])
            q.wait(sld2["A"])
            q.wait(sld2["D"])
            q.wait(asp2["A"])
            q.wait(asp2["D"])
            q.wait(S.hmsp)
            q.wait(hmld2[0])
            q.wait(hmld2[1])

        fl_hist = {}
        mm3_hist = {}
        xg_hist = {}
        for hpi, (hoff, hw) in enumerate(cfg.hpanels):
            for gi, (ic0, gn) in enumerate(cfg.groups3):
                gseq = hpi * NGRP + gi
                par = gseq % 2
                prev = divmod(gseq - 2, NGRP) if gseq - 2 >= 0 else None
                if prev is not None and (prev[0], prev[1], TT - 1) in fl_hist:
                    pv = fl_hist[(prev[0], prev[1], TT - 1)]
                    SCQ.wait(S.fl, pv)
                    SY.wait(S.fl, pv)
                    GP.wait(S.fl, pv)
                for c in range(gn):
                    ic = ic0 + c
                    slot = hmT[:, (par * cfg.G3 + c) * T:
                               (par * cfg.G3 + c) * T + T]
                    SCQ.dma(trh2[par], slot, hm_d[:, ic * 128:(ic + 1) * 128],
                            transpose=True)
                    GP.dma(wdld2[par], wdr[:, (par * cfg.G3 + c) * HPW:
                                           (par * cfg.G3 + c) * HPW + hw],
                           wdt_d[ic, :, hoff:hoff + hw])
                TE.wait(trh2[par])
                TE.wait(wdld2[par])
                for ti in range(TT):
                    pp = psum[ti % 4]
                    key = (hpi, gi, ti)
                    pk = _prev_psum_use(key, NGRP, TT)
                    if pk is not None and pk in fl_hist:
                        TE.wait(S.fl, fl_hist[pk])
                    nsub = (hw + 511) // 512
                    for c in range(gn):
                        lhs = hmT[:, (par * cfg.G3 + c) * T + ti * 128:
                                  (par * cfg.G3 + c) * T + (ti + 1) * 128]
                        for si, sub in enumerate(range(0, hw, 512)):
                            sw = min(512, hw - sub)
                            last = (c == gn - 1 and si == nsub - 1)
                            TE.run("matmul", pp[:, sub:sub + sw], lhs,
                                   wdr[:, (par * cfg.G3 + c) * HPW + sub:
                                       (par * cfg.G3 + c) * HPW + sub + sw],
                                   start=(c == 0), stop=(c == gn - 1),
                                   sem=(S.mm1 if last else None))
                    mm3_hist[(hpi, gi, ti)] = S.mm1.v
                    VE.wait(S.mm1, mm3_hist[(hpi, gi, ti)])
                    od = outs[:, ti * HPW: ti * HPW + hw]
                    if gi == 0:
                        if hpi > 0:
                            VE.wait(S.fl, fl_hist[(hpi - 1, NGRP - 1, ti)])
                            VE.wait(S.outp)
                        VE.run("tensor_copy", od, pp[:, 0:hw], sem=S.fl)
                    else:
                        VE.wait(S.fl, fl_hist[(hpi, gi - 1, ti)])
                        VE.run("tensor_tensor", out=od, in0=od, in1=pp[:, 0:hw],
                               op=aop.add, sem=S.fl)
                    fl_hist[(hpi, gi, ti)] = S.fl.v
                    if gi == NGRP - 1:
                        sl = ti % 2
                        xs = xg[:, sl * HPW: sl * HPW + hw]
                        gs_ = xg[:, (2 + sl) * HPW: (2 + sl) * HPW + hw]
                        if (hpi, ti - 2) in xg_hist:
                            SY.wait(S.fl, xg_hist[(hpi, ti - 2)])
                        elif hpi > 0 and (hpi - 1, ti + TT - 2) in xg_hist:
                            SY.wait(S.fl, xg_hist[(hpi - 1, ti + TT - 2)])
                        SY.dma(xgld2[sl], xs,
                               x_d[ti * 128:(ti + 1) * 128, hoff:hoff + hw])
                        SY.dma(xgld2[sl], gs_,
                               g_d[ti * 128:(ti + 1) * 128, hoff:hoff + hw])
                        VE.wait(xgld2[sl])
                        VE.wait(S.fl)
                        VE.run("tensor_tensor", out=od, in0=od, in1=gs_,
                               op=aop.mult, sem=S.fl)
                        VE.wait(S.fl)
                        VE.run("tensor_tensor", out=od, in0=od, in1=xs,
                               op=aop.add, sem=S.fl)
                        xg_hist[(hpi, ti)] = S.fl.v
                        fl_hist[(hpi, gi, ti)] = S.fl.v
                        SY.wait(S.fl, S.fl.v)
                        SY.dma(S.outp, out_d[ti * 128:(ti + 1) * 128, hoff:hoff + hw],
                               od)
        SY.wait(S.outp, S.outp.v)
        qs["gpsimd"].wait(S.outp, S.outp.v)
    flush_block(nc, "p3b", qs)

    st.close()
    return nc


def _prev_psum_use(key, ngrp, tt):
    hpi, gi, ti = key
    ti2 = ti - 4
    if ti2 >= 0:
        return (hpi, gi, ti2)
    gi2 = gi - 1
    if gi2 >= 0:
        return (hpi, gi2, tt + ti2)
    if hpi > 0:
        return (hpi - 1, ngrp - 1, tt + ti2)
    return None


def _host_pack(cfg, w_gate, w_up, w_down, packed, scales):
    H, I, O, KC, BLK, IC = cfg.H, cfg.I, cfg.O, cfg.KC, cfg.BLK, cfg.IC
    blockperm = np.array([2 * (p % 64) + (p // 64) for p in range(128)])
    sig = (np.arange(H) // 128) * 128 + blockperm[np.arange(H) % 128]

    def pack_w(w):
        wt = np.ascontiguousarray(w.T[sig])
        pieces = []
        for (poff, W) in cfg.panels1:
            blk = wt[:, poff:poff + W].reshape(KC, 128, W)
            for g in range(cfg.NGR):
                ck = min(cfg.GK, KC - g * cfg.GK)
                pieces.append(np.ascontiguousarray(blk[g * cfg.GK:g * cfg.GK + ck]).ravel())
        return np.concatenate(pieces)

    wg_flat = pack_w(np.asarray(w_gate, np.float32))
    wu_flat = pack_w(np.asarray(w_up, np.float32))
    wdt = np.ascontiguousarray(np.asarray(w_down, np.float32).T).reshape(IC, 128, O)
    pkt = np.ascontiguousarray(np.asarray(packed, np.int32).transpose(1, 2, 0))
    sc = np.asarray(scales, np.float32)[:, :, 0].T
    scb = np.ascontiguousarray(np.broadcast_to(sc[:, None, :], (BLK, 128, O)))
    return wg_flat, wu_flat, wdt, pkt, scb


def kernel(L0_attn_x, L0_attn_w_gate, L0_attn_w_up, L0_attn_w_down,
           L0_ffn_packed_weight, L0_ffn_scales):
    cfg = Cfg()
    x = np.asarray(L0_attn_x, np.float32)
    B = x.shape[0]
    wg_flat, wu_flat, wdt, pkt, scb = _host_pack(
        cfg, L0_attn_w_gate, L0_attn_w_up, L0_attn_w_down,
        L0_ffn_packed_weight, L0_ffn_scales)
    nc = build_nc(cfg)
    in_maps = []
    for c in range(B):
        in_maps.append(dict(x=np.ascontiguousarray(x[c]), wg=wg_flat, wu=wu_flat,
                            wdt=wdt, pkt=pkt, scb=scb))
    res = run_bass_kernel_spmd(nc, in_maps, core_ids=list(range(B)))
    out = np.stack([np.asarray(r["out"], np.float32) for r in res.results])
    return out


# version 12
# speedup vs baseline: 1.1909x; 1.0609x over previous
import contextlib
import numpy as np
import ml_dtypes
import concourse.bass as bass
import concourse.mybir as mybir
from concourse.bass_utils import run_bass_kernel_spmd
from concourse.alu_op_type import AluOpType as aop

F32 = mybir.dt.float32
F16 = mybir.dt.float16
BF16 = mybir.dt.bfloat16
I32 = mybir.dt.int32
F8 = mybir.dt.float8e4
AF = mybir.ActivationFunctionType
AX = mybir.AxisListType
bfnp = ml_dtypes.bfloat16


class Cfg:
    def __init__(self, T=2048, H=4096, I=11008, O=4096, NIT=14, n_cores=8):
        self.T, self.H, self.I, self.O = T, H, I, O
        self.NIT = NIT
        self.n_cores = n_cores
        self.TT = T // 128
        self.KC = H // 128
        self.IC = I // 128
        self.BLK = H // 128
        self.K = I // 2
        self.W1 = 384
        ws = []
        off = 0
        while off < I:
            w = min(self.W1, I - off)
            ws.append((off, w))
            off += w
        self.panels1 = ws
        self.GK = 4
        self.NGR = (self.KC + self.GK - 1) // self.GK
        self.TG = 2
        assert self.TT % self.TG == 0
        assert self.TT <= 16
        self.W2 = 512 if O >= 512 else O
        assert O % self.W2 == 0
        self.OP = O // self.W2
        self.TH2 = (self.TT + 7) // 8
        self.BB = 2
        assert self.BLK % self.BB == 0
        self.G3 = 8
        gs = []
        ic = 0
        while ic < self.IC:
            g = min(self.G3, self.IC - ic)
            gs.append((ic, g))
            ic += g
        self.groups3 = gs
        self.HP = 1024 if O >= 1024 else O
        hp = []
        off = 0
        while off < O:
            w = min(self.HP, O - off)
            hp.append((off, w))
            off += w
        self.hpanels = hp
        self.TH0 = 8.0
        self.W0 = 4.0


class Sem:
    def __init__(self, st, nc, name):
        self.h = st.enter_context(nc.semaphore(name))
        self.v = 0


class Q:

    def __init__(self, name):
        self.name = name
        self.ops = []
        self._lastwait = {}

    def add(self, f):
        self.ops.append(f)

    def wait(self, sem, val=None):
        v = sem.v if val is None else val
        if v <= 0:
            return
        if self._lastwait.get(id(sem), -1) >= v:
            return
        self._lastwait[id(sem)] = v
        self.ops.append(lambda e, h=sem.h, v=v: e.wait_ge(h, v))

    @staticmethod
    def _ap(x):
        if isinstance(x, bass.AP) or not hasattr(x, "ap"):
            return x
        return x.ap()

    def dma(self, sem, out, in_, **kw):
        sem.v += 16
        out, in_ = self._ap(out), self._ap(in_)
        def f(e, out=out, in_=in_, kw=kw, h=sem.h):
            e.dma_start(out=out, in_=in_, **kw).then_inc(h, 16)
        self.ops.append(f)

    def run(self, fname, *args, sem=None, inc=1, **kw):
        if sem is not None:
            sem.v += inc
        args = tuple(self._ap(a) for a in args)
        kw = {k: (self._ap(v) if k in ("out", "in_", "in0", "in1") else v)
              for k, v in kw.items()}
        def f(e, fname=fname, args=args, kw=kw, sem=sem, inc=inc):
            r = getattr(e, fname)(*args, **kw)
            if sem is not None:
                r.then_inc(sem.h, inc)
        self.ops.append(f)


def flush_block(nc, name, qs):
    with nc.Block(name) as blk:
        for q in qs.values():
            if q.ops:
                runner = getattr(blk, q.name)
                def body(eng, ops=q.ops):
                    for f in ops:
                        f(eng)
                runner(body)


def new_queues():
    return {n: Q(n) for n in ("sync", "scalar", "vector", "gpsimd", "tensor")}


def build_nc(cfg: Cfg):
    T, H, I, O = cfg.T, cfg.H, cfg.I, cfg.O
    TT, KC, IC, BLK = cfg.TT, cfg.KC, cfg.IC, cfg.BLK
    nc = bass.Bass("TRN2", target_bir_lowering=False)
    st = contextlib.ExitStack()

    x_d = nc.dram_tensor("x", [T, H], F32, kind="ExternalInput").ap()
    ng = 4 * 128
    wg_sz = H * I
    wg_d = nc.dram_tensor("wg", [wg_sz], F32, kind="ExternalInput").ap()
    wu_d = nc.dram_tensor("wu", [wg_sz], F32, kind="ExternalInput").ap()
    wdt_d = nc.dram_tensor("wdt", [IC, 128, O], F32, kind="ExternalInput").ap()
    pkt_d = nc.dram_tensor("pkt", [BLK, 64, O], I32, kind="ExternalInput").ap()
    scb_d = nc.dram_tensor("scb", [BLK, 128, O], F32, kind="ExternalInput").ap()
    out_d = nc.dram_tensor("out", [T, O], F32, kind="ExternalOutput").ap()
    xn_d = nc.dram_tensor("xn_s", [T, H], BF16, kind="Internal").ap()
    s_d = nc.dram_tensor("s_s", [TT, 128, I], F16, kind="Internal").ap()
    sup_d = nc.dram_tensor("sup_s", [TT, 128, I], BF16, kind="Internal").ap()
    mask_d = nc.dram_tensor("mask_s", [TT, 128, I], F8, kind="Internal").ap()
    hm_d = nc.dram_tensor("hm_s", [T, I], BF16, kind="Internal").ap()
    g_d = nc.dram_tensor("g_s", [T, O], F32, kind="Internal").ap()
    wq_dram = nc.dram_tensor("wq_s", [BLK, 128, O], BF16, kind="Internal").ap()

    gran_off = {}
    off = 0
    for p, (poff, W) in enumerate(cfg.panels1):
        for g in range(cfg.NGR):
            ck = min(cfg.GK, KC - g * cfg.GK)
            gran_off[(p, g)] = (off, ck, W)
            off += ck * 128 * W
    assert off == wg_sz

    sems = {}
    for name in ("xld", "v0", "a0", "spxn", "trx",
                 "pegr", "mm1", "e1a", "e1v", "sp1",
                 "pkld", "scld", "dq", "evg", "gsp", "peb",
                 "sld", "chA", "chD", "mk", "msp",
                 "hmld", "hmtt", "hmsp", "trh", "wdld", "xgld", "fl", "outp",
                 "wqsp"):
        sems[name] = Sem(st, nc, name)
    S = type("S", (), sems)
    wld = [Sem(st, nc, f"wld{g}") for g in range(cfg.NGR)]
    xld2 = [Sem(st, nc, f"xld{i}") for i in range(2)]
    pkld2 = [Sem(st, nc, f"pkld{i}") for i in range(2)]
    scld2 = [Sem(st, nc, f"scld{i}") for i in range(2)]
    sld2 = {c: Sem(st, nc, f"sld{c}") for c in "AD"}
    msp2 = {c: Sem(st, nc, f"msp{c}") for c in "AD"}
    hmld2 = [Sem(st, nc, f"hmld{i}") for i in range(2)]
    trh2 = [Sem(st, nc, f"trh{i}") for i in range(2)]
    wdld2 = [Sem(st, nc, f"wdld{i}") for i in range(2)]
    xgld2 = [Sem(st, nc, f"xgld{i}") for i in range(2)]
    wqld2 = [Sem(st, nc, f"wqld{i}") for i in range(2)]
    gsp2 = [Sem(st, nc, f"gsp{i}") for i in range(2)]
    hmld6 = [Sem(st, nc, f"hmld6_{i}") for i in range(6)]
    asp2 = {c: Sem(st, nc, f"asp{c}") for c in "AD"}

    stats = st.enter_context(nc.sbuf_tensor("stats", [128, 16], F32))
    th_a = st.enter_context(nc.sbuf_tensor("th_a", [128, TT], F32))
    th_b = st.enter_context(nc.sbuf_tensor("th_b", [128, TT], F32))
    cnt_t = st.enter_context(nc.sbuf_tensor("cnt_t", [128, TT], F32))
    sgn_t = st.enter_context(nc.sbuf_tensor("sgn_t", [128, TT], F32))
    xnT_ctx = nc.sbuf_tensor("xnT", [128, KC * T], BF16)
    xnT = xnT_ctx.__enter__()

    qs = new_queues()
    with (nc.sbuf_tensor("xbuf", [128, 2 * H], F32) as xbuf,
          nc.sbuf_tensor("xnst", [128, 2 * H], BF16) as xnst,
          nc.sbuf_tensor("junk0", [128, H], F8) as junk8):
        SY, SC, VE = qs["sync"], qs["scalar"], qs["vector"]
        rel_x = {}
        for tt in range(TT):
            par = tt % 2
            xb = xbuf[:, par * H:(par + 1) * H]
            xs = xnst[:, par * H:(par + 1) * H]
            if tt - 2 >= 0:
                for sem, val in rel_x[tt - 2]:
                    SY.wait(sem, val)
            SY.dma(xld2[par], xb, x_d[tt * 128:(tt + 1) * 128, :])
            c = (tt % 2) * 8
            VE.wait(xld2[par])
            VE.run("reduce_sum", stats[:, c:c + 1], xb, axis=AX.X, sem=S.v0)
            VE.wait(S.v0)
            VE.run("tensor_scalar", out=stats[:, c + 1:c + 2], in0=stats[:, c:c + 1],
                   scalar1=-1.0 / H, scalar2=None, op0=aop.mult, sem=S.v0)
            SC.wait(S.v0)
            SC.wait(xld2[par], xld2[par].v)
            SC.run("activation", out=junk8[:, :H], in_=xb, func=AF.Square,
                   bias=stats[:, c + 1:c + 2], scale=1.0,
                   accum_out=stats[:, c + 2:c + 3], sem=S.a0)
            VE.wait(S.a0)
            VE.run("tensor_scalar", out=stats[:, c + 3:c + 4], in0=stats[:, c + 2:c + 3],
                   scalar1=1.0 / H, scalar2=1e-5, op0=aop.mult, op1=aop.add, sem=S.v0)
            SC.wait(S.v0)
            SC.run("activation", out=stats[:, c + 4:c + 5], in_=stats[:, c + 3:c + 4],
                   func=AF.Sqrt, sem=S.a0)
            VE.wait(S.a0)
            VE.run("reciprocal", stats[:, c + 5:c + 6], stats[:, c + 4:c + 5], sem=S.v0)
            VE.wait(S.v0)
            if tt - 2 >= 0:
                VE.wait(S.spxn)
            xs_perm = xs.rearrange("p (b r c) -> p b c r", b=KC, r=2)
            VE.run("tensor_scalar", out=xs_perm, in0=xb,
                   scalar1=stats[:, c + 1:c + 2], scalar2=stats[:, c + 5:c + 6],
                   op0=aop.add, op1=aop.mult, sem=S.v0)
            rel_x[tt] = [(S.a0, S.a0.v), (S.v0, S.v0.v)]
            SY.wait(S.v0)
            SY.dma(S.spxn, xn_d[tt * 128:(tt + 1) * 128, :], xs)
        SCQ = qs["scalar"]
        SCQ.wait(S.spxn)
        for k in range(KC):
            SCQ.dma(S.trx, xnT[:, k * T:(k + 1) * T],
                    xn_d[:, k * 128:(k + 1) * 128], transpose=True)
    flush_block(nc, "p0", qs)

    qs = new_queues()
    slotW = cfg.GK * cfg.W1
    with (nc.sbuf_tensor("wgr", [128, cfg.NGR * slotW], BF16) as wgr,
          nc.sbuf_tensor("wur", [128, cfg.NGR * slotW], BF16) as wur,
          nc.sbuf_tensor("sst", [128, 2 * cfg.TG * cfg.W1], F16) as sst,
          nc.sbuf_tensor("supst", [128, 2 * cfg.TG * cfg.W1], BF16) as supst,
          nc.sbuf_tensor("pkq", [128, 2 * 1024], I32) as pkq,
          nc.sbuf_tensor("nibq", [128, 1024], I32) as nibq,
          nc.sbuf_tensor("scq", [128, 2 * 1024], F32) as scq,
          nc.sbuf_tensor("wqst", [128, 2 * 1024], BF16) as wqst,
          contextlib.ExitStack() as pst):
        psum = [pst.enter_context(nc.psum_tensor(f"p1_{i}", [128, 512], F32))
                for i in range(8)]
        SY, SC, VE, GP, TE = (qs[n] for n in ("sync", "scalar", "vector", "gpsimd", "tensor"))
        TE.wait(S.trx)
        NGROUP = TT // cfg.TG
        npan = len(cfg.panels1)
        ev_hist = {}
        sp_hist = {}
        gq_hist = {}
        def load_gran(p, g):
            offs, ck, W = gran_off[(p, g)]
            if p > 0:
                if g < cfg.NGR - 1:
                    GP.wait(S.pegr, (p - 1) * (cfg.NGR - 1) + g + 1)
                else:
                    GP.wait(S.mm1, p * NGROUP)
            for (ring, src) in ((wgr, wg_d), (wur, wu_d)):
                dst = ring[:, g * slotW + 0: g * slotW + ck * W]
                dst = dst.rearrange("p (c w) -> p c w", c=ck)
                srcap = src[offs: offs + ck * 128 * W]
                srcap = srcap.rearrange("(c p w) -> p c w", c=ck, p=128)
                GP.dma(wld[g], dst, srcap)
        for g in range(cfg.NGR):
            load_gran(0, g)
        DCW = min(512, O)
        batches = [(b0, c0) for b0 in range(0, BLK, 2) for c0 in range(0, O, DCW)]
        NBAT = len(batches)
        dqb_hist = {}
        deq_i = [0]

        def deq_step():
            bi = deq_i[0]
            if bi >= NBAT:
                return
            deq_i[0] += 1
            b0, c0 = batches[bi]
            par = bi % 2
            bw2 = 2 * DCW
            if bi - 2 >= 0:
                SY.wait(S.dq, dqb_hist[bi - 2])
            src = pkt_d[b0:b0 + 2, :, c0:c0 + DCW].rearrange("b p w -> p b w")
            SY.dma(pkld2[par], pkq[0:64, par * bw2:(par + 1) * bw2].rearrange(
                "p (b w) -> p b w", b=2), src)
            SY.dma(pkld2[par], pkq[64:128, par * bw2:(par + 1) * bw2].rearrange(
                "p (b w) -> p b w", b=2), src)
            scsrc = scb_d[b0:b0 + 2, :, c0:c0 + DCW].rearrange("b p w -> p b w")
            SY.dma(scld2[par], scq[:, par * bw2:(par + 1) * bw2].rearrange(
                "p (b w) -> p b w", b=2), scsrc)
            VE.wait(pkld2[par])
            VE.wait(S.dq)
            VE.wait(S.wqsp)
            VE.run("tensor_scalar", out=nibq[0:64, :bw2],
                   in0=pkq[0:64, par * bw2:(par + 1) * bw2], scalar1=4,
                   scalar2=None, op0=aop.logical_shift_right, sem=S.dq)
            VE.run("tensor_scalar", out=nibq[64:128, :bw2],
                   in0=pkq[64:128, par * bw2:(par + 1) * bw2], scalar1=15,
                   scalar2=None, op0=aop.bitwise_and, sem=S.dq)
            VE.wait(scld2[par])
            VE.wait(S.dq)
            VE.run("scalar_tensor_tensor", out=wqst[:, par * bw2:(par + 1) * bw2],
                   in0=nibq[:, :bw2], scalar=8.0,
                   in1=scq[:, par * bw2:(par + 1) * bw2], op0=aop.subtract,
                   op1=aop.mult, sem=S.dq)
            dqb_hist[bi] = S.dq.v
            GP.wait(S.dq, dqb_hist[bi])
            GP.dma(S.wqsp, wq_dram[b0, :, c0:c0 + DCW],
                   wqst[:, par * bw2: par * bw2 + DCW])
            GP.dma(S.wqsp, wq_dram[b0 + 1, :, c0:c0 + DCW],
                   wqst[:, par * bw2 + DCW:(par + 1) * bw2])

        gidx = 0
        ngroups_total = len(cfg.panels1) * NGROUP
        for p, (poff, W) in enumerate(cfg.panels1):
            for tg in range(NGROUP):
                par = tg % 2
                tiles = (cfg.TG * tg, cfg.TG * tg + 1)
                if gidx - 2 >= 0:
                    ea, ev = ev_hist[gidx - 2]
                    TE.wait(S.e1a, ea)
                    TE.wait(S.e1v, ev)
                for k in range(KC):
                    g = k // cfg.GK
                    if tg == 0:
                        TE.wait(wld[g], 32 * (p + 1))
                    ck_local = k % cfg.GK
                    woff = g * slotW + ck_local * W
                    for tl, ti in enumerate(tiles):
                        lhs = xnT[:, k * T + ti * 128: k * T + (ti + 1) * 128]
                        pg = psum[par * 4 + 2 * tl][:, 0:W]
                        pu = psum[par * 4 + 2 * tl + 1][:, 0:W]
                        kw = dict(start=(k == 0), stop=(k == KC - 1))
                        last_of_group = (k == KC - 1 and tl == len(tiles) - 1)
                        rel_gran = (tg == NGROUP - 1 and ck_local == 0 and g > 0
                                    and tl == 0)
                        if rel_gran:
                            TE.run("matmul", pg, lhs, wgr[:, woff:woff + W],
                                   sem=S.pegr, **kw)
                            if p + 1 < npan:
                                load_gran(p + 1, g - 1)
                        else:
                            TE.run("matmul", pg, lhs, wgr[:, woff:woff + W], **kw)
                        if last_of_group:
                            TE.run("matmul", pu, lhs, wur[:, woff:woff + W],
                                   sem=S.mm1, **kw)
                        else:
                            TE.run("matmul", pu, lhs, wur[:, woff:woff + W], **kw)
                gq_hist[gidx] = S.mm1.v
                if tg == NGROUP - 1 and p + 1 < npan:
                    load_gran(p + 1, cfg.NGR - 1)
                SC.wait(S.mm1, gq_hist[gidx])
                if gidx - 2 >= 0:
                    SC.wait(S.sp1)
                for tl, ti in enumerate(tiles):
                    sslice = sst[:, (par * cfg.TG + tl) * cfg.W1:
                                 (par * cfg.TG + tl) * cfg.W1 + W]
                    SC.run("activation", out=sslice, in_=psum[par * 4 + 2 * tl][:, 0:W],
                           func=AF.Silu, sem=S.e1a)
                ev_a = S.e1a.v
                VE.wait(S.e1a, ev_a)
                if gidx - 2 >= 0:
                    VE.wait(S.sp1)
                for tl, ti in enumerate(tiles):
                    sslice = sst[:, (par * cfg.TG + tl) * cfg.W1:
                                 (par * cfg.TG + tl) * cfg.W1 + W]
                    supslice = supst[:, (par * cfg.TG + tl) * cfg.W1:
                                     (par * cfg.TG + tl) * cfg.W1 + W]
                    VE.run("tensor_tensor", out=supslice, in0=sslice,
                           in1=psum[par * 4 + 2 * tl + 1][:, 0:W], op=aop.mult,
                           sem=S.e1v)
                ev_v = S.e1v.v
                ev_hist[gidx] = (ev_a, ev_v)
                SY.wait(S.e1a, ev_a)
                SY.wait(S.e1v, ev_v)
                for tl, ti in enumerate(tiles):
                    sslice = sst[:, (par * cfg.TG + tl) * cfg.W1:
                                 (par * cfg.TG + tl) * cfg.W1 + W]
                    supslice = supst[:, (par * cfg.TG + tl) * cfg.W1:
                                     (par * cfg.TG + tl) * cfg.W1 + W]
                    SY.dma(S.sp1, s_d[ti, :, poff:poff + W], sslice)
                    SY.dma(S.sp1, sup_d[ti, :, poff:poff + W], supslice)
                sp_hist[gidx] = S.sp1.v
                gidx += 1
                while deq_i[0] * ngroups_total < NBAT * gidx:
                    deq_step()
        while deq_i[0] < NBAT:
            deq_step()
    flush_block(nc, "p1", qs)

    qs = new_queues()
    IH = (I + 1) // 2
    QH = min(640, I)
    HR = 6
    W2 = cfg.W2
    with (nc.sbuf_tensor("abufA", [128, I], F16) as abufA,
          nc.sbuf_tensor("abufD", [128, I], F16) as abufD,
          nc.sbuf_tensor("junkA", [128, IH], F8) as junkA,
          nc.sbuf_tensor("junkD", [128, IH], F8) as junkD,
          nc.sbuf_tensor("supq", [128, 6 * QH], BF16) as supq,
          nc.sbuf_tensor("asq", [128, 6 * QH], F16) as asq,
          nc.sbuf_tensor("wqr", [128, 2 * 2 * W2], BF16) as wqr,
          nc.sbuf_tensor("gst", [128, 2 * W2], F32) as gst,
          contextlib.ExitStack() as pst):
        psum = [pst.enter_context(nc.psum_tensor(f"p2_{i}", [128, 512], F32))
                for i in range(8)]
        SY, SC, VE, GP, TE = (qs[n] for n in ("sync", "scalar", "vector", "gpsimd", "tensor"))
        for q in (SY, SC, VE, GP):
            q.wait(S.sp1)
            q.wait(S.wqsp)
        TE.wait(S.trx)

        US_MM = 0.00022 * 512
        tPE = 0.0
        evac_pts = []
        peb_hist = {}
        NHALF = cfg.TH2
        halves = [list(range(h * 8, min(TT, h * 8 + 8))) for h in range(NHALF)]
        nsilu_per = [len(h) for h in halves]
        evg_after = {}
        cum = 0
        for op_ in range(cfg.OP):
            for th_ in range(NHALF):
                cum += nsilu_per[th_]
                evg_after[(op_, th_)] = cum
        for op_ in range(cfg.OP):
            ocols = slice(op_ * W2, (op_ + 1) * W2)
            for th_ in range(NHALF):
                tiles = halves[th_]
                for b0 in range(0, BLK, 2):
                    gpar = (b0 // 2) % 2
                    seqg = (op_ * NHALF + th_) * (BLK // 2) + b0 // 2
                    if seqg - 2 >= 0:
                        GP.wait(S.peb, peb_hist[seqg - 2])
                    GP.dma(wqld2[gpar],
                           wqr[:, gpar * 2 * W2:(gpar + 1) * 2 * W2].rearrange(
                               "p (b w) -> p b w", b=2),
                           wq_dram[b0:b0 + 2, :, ocols].rearrange("b p w -> p b w"))
                    TE.wait(wqld2[gpar])
                    if op_ > 0 and b0 == 0 and th_ == 0:
                        TE.wait(S.evg, evg_after[(op_ - 1, NHALF - 1)])
                    if th_ == 1 and b0 == 0:
                        TE.wait(S.evg, evg_after[(op_, 0)])
                    for bl in range(2):
                        b = b0 + bl
                        rhs = wqr[:, (gpar * 2 + bl) * W2:(gpar * 2 + bl + 1) * W2]
                        for n_, ti in enumerate(tiles):
                            lhs = xnT[:, b * T + ti * 128: b * T + (ti + 1) * 128]
                            pq = psum[n_][:, 0:W2]
                            TE.run("matmul", pq, lhs, rhs,
                                   start=(b == 0), stop=(b == BLK - 1),
                                   sem=(S.peb if (bl == 1 and n_ == len(tiles) - 1)
                                        else None))
                        tPE += len(tiles) * US_MM
                    peb_hist[seqg] = S.peb.v
                evac_pts.append((S.peb.v, op_, th_, tiles, tPE))

        alltiles = list(range(TT))
        NA = min((9 * TT + 15) // 16, TT)
        tiles_A = alltiles[:NA]
        tiles_D = alltiles[NA:]
        Kh = cfg.K + 0.5
        VE.run("memset", th_a[:, :], cfg.TH0, sem=S.chD)
        th_bufs = [th_a, th_b]
        chA_after = {}
        opsA = 1 + 5 * cfg.NIT
        for n in range(len(tiles_A)):
            chA_after[n] = (n + 1) * opsA
        sldA_hist = {}
        aspA_hist = {}
        aspD_hist = {}
        chD_after = {}
        hq_seq = [0]
        hq_tt = {}

        hm_units = []
        hq_ld = {}

        def hmid_load(u):
            j, c0, cw, chs, chv, asps, aspv = hm_units[u]
            par = u % HR
            if u - HR >= 0:
                SY.wait(S.mk, hq_tt[u - HR])
                SY.wait(S.hmsp)
            SY.wait(asps, aspv)
            SY.dma(hmld6[par], supq[:, par * QH: par * QH + cw],
                   sup_d[j, :, c0:c0 + cw])
            SY.dma(hmld6[par], asq[:, par * QH: par * QH + cw],
                   s_d[j, :, c0:c0 + cw])
            hq_ld[u] = hmld6[par].v

        def hmid_tail():
            fin = th_bufs[cfg.NIT % 2]
            for u in range(min(HR, len(hm_units))):
                hmid_load(u)
            for u in range(len(hm_units)):
                j, c0, cw, chs, chv, asps, aspv = hm_units[u]
                par = u % HR
                sq = supq[:, par * QH: par * QH + cw]
                aq = asq[:, par * QH: par * QH + cw]
                VE.wait(hmld6[par], hq_ld[u])
                VE.wait(chs, chv)
                if u - HR >= 0:
                    VE.wait(S.hmsp)
                VE.run("scalar_tensor_tensor", out=sq, in0=aq,
                       scalar=fin[:, j:j + 1], in1=sq, op0=aop.is_gt,
                       op1=aop.mult, sem=S.mk)
                hq_tt[u] = S.mk.v
                SY.wait(S.mk, hq_tt[u])
                SY.dma(S.hmsp, hm_d[j * 128:(j + 1) * 128, c0:c0 + cw], sq)
                if u + HR < len(hm_units):
                    hmid_load(u + HR)

        def hmid_tile(j, chs, chv, asps, aspv):
            for c0 in range(0, I, QH):
                hm_units.append((j, c0, min(QH, I - c0), chs, chv, asps, aspv))

        def bisect_D(j, abuf, jbuf):
            VE.wait(sld2["D"])
            VE.run("scalar_tensor_tensor", out=abuf, in0=abuf, scalar=-1.0,
                   in1=abuf, op0=aop.mult, op1=aop.max, sem=S.chD)
            w = cfg.W0
            for it in range(cfg.NIT):
                cur = th_bufs[it % 2]
                nxt = th_bufs[(it + 1) % 2]
                VE.wait(S.chD)
                VE.run("tensor_scalar", out=jbuf, in0=abuf[:, 0:IH],
                       scalar1=cur[:, j:j + 1], scalar2=None, op0=aop.is_gt,
                       op1=aop.add, accum_out=cnt_t[:, j:j + 1], sem=S.chD)
                VE.wait(S.chD)
                VE.run("tensor_scalar", out=jbuf[:, 0:I - IH], in0=abuf[:, IH:I],
                       scalar1=cur[:, j:j + 1], scalar2=None, op0=aop.is_gt,
                       op1=aop.add, accum_out=sgn_t[:, j:j + 1], sem=S.chD)
                VE.wait(S.chD)
                VE.run("tensor_tensor", out=cnt_t[:, j:j + 1],
                       in0=cnt_t[:, j:j + 1], in1=sgn_t[:, j:j + 1],
                       op=aop.add, sem=S.chD)
                VE.wait(S.chD)
                VE.run("tensor_scalar", out=sgn_t[:, j:j + 1],
                       in0=cnt_t[:, j:j + 1], scalar1=float(Kh), scalar2=None,
                       op0=aop.is_gt, sem=S.chD)
                VE.wait(S.chD)
                VE.run("tensor_scalar", out=sgn_t[:, j:j + 1],
                       in0=sgn_t[:, j:j + 1], scalar1=float(2 * w),
                       scalar2=float(w), op0=aop.mult, op1=aop.subtract,
                       sem=S.chD)
                VE.wait(S.chD)
                VE.run("tensor_tensor", out=nxt[:, j:j + 1],
                       in0=cur[:, j:j + 1], in1=sgn_t[:, j:j + 1],
                       op=aop.add, sem=S.chD)
                w *= 0.5

        NT = max(len(tiles_A), len(tiles_D))
        for n in range(NT):
            if n < len(tiles_A):
                j = tiles_A[n]
                if n - 1 >= 0:
                    SY.wait(asp2["A"])
                    SY.wait(S.chA, chA_after[n - 1])
                SY.dma(sld2["A"], abufA, s_d[j, :, :])
                sldA_hist[n] = sld2["A"].v
                SY.wait(S.chA, n * opsA + 1)
                SY.dma(asp2["A"], s_d[j, :, :], abufA)
                aspA_hist[n] = asp2["A"].v
            if n < len(tiles_D):
                j = tiles_D[n]
                if n - 1 >= 0:
                    SY.wait(asp2["D"])
                    SY.wait(S.chD, S.chD.v)
                SY.dma(sld2["D"], abufD, s_d[j, :, :])
                bisect_D(j, abufD, junkD)
                SY.wait(S.chD, S.chD.v - 6 * cfg.NIT)
                SY.dma(asp2["D"], s_d[j, :, :], abufD)
                chD_after[n] = S.chD.v
                aspD_hist[n] = asp2["D"].v
            if n - 1 >= 0:
                if n - 1 < len(tiles_A):
                    hmid_tile(tiles_A[n - 1], S.chA, chA_after[n - 1],
                              asp2["A"], aspA_hist[n - 1])
                if n - 1 < len(tiles_D):
                    hmid_tile(tiles_D[n - 1], S.chD, chD_after[n - 1],
                              asp2["D"], aspD_hist[n - 1])
            if n == max(len(tiles_A), len(tiles_D)) - 1:
                m = n
                if m < len(tiles_A):
                    hmid_tile(tiles_A[m], S.chA, chA_after[m],
                              asp2["A"], aspA_hist[m])
                if m < len(tiles_D):
                    hmid_tile(tiles_D[m], S.chD, chD_after[m],
                              asp2["D"], aspD_hist[m])
        hmid_tail()

        a_steps = []
        for n, j in enumerate(tiles_A):

            def start_step(n=n, j=j):
                SC.wait(sld2["A"], sldA_hist[n])
                SC.wait(S.chD, 1)
                SC.run("activation", out=abufA, in_=abufA, func=AF.Abs, sem=S.chA)
            a_steps.append((10.0, start_step))
            w = cfg.W0
            for it in range(cfg.NIT):
                cur = th_bufs[it % 2]
                nxt = th_bufs[(it + 1) % 2]

                def iter_step(j=j, cur=cur, nxt=nxt, w=w):
                    SC.wait(S.chA)
                    SC.run("activation", out=junkA, in_=abufA[:, 0:IH],
                           func=AF.Sign, bias=cur[:, j:j + 1], scale=-1.0,
                           accum_out=cnt_t[:, j:j + 1], sem=S.chA)
                    SC.wait(S.chA)
                    SC.run("activation", out=junkA[:, 0:I - IH],
                           in_=abufA[:, IH:I], func=AF.Sign,
                           bias=cur[:, j:j + 1], scale=-1.0,
                           accum_out=sgn_t[:, j:j + 1], sem=S.chA)
                    SC.wait(S.chA)
                    SC.run("activation", out=cnt_t[:, j:j + 1],
                           in_=cnt_t[:, j:j + 1], func=AF.Identity, scale=1.0,
                           bias=sgn_t[:, j:j + 1], sem=S.chA)
                    SC.wait(S.chA)
                    SC.run("activation", out=sgn_t[:, j:j + 1],
                           in_=cnt_t[:, j:j + 1], func=AF.Sign, scale=-1.0,
                           bias=0.0, sem=S.chA)
                    SC.wait(S.chA)
                    SC.run("activation", out=nxt[:, j:j + 1],
                           in_=sgn_t[:, j:j + 1], func=AF.Identity, scale=float(w),
                           bias=cur[:, j:j + 1], sem=S.chA)
                a_steps.append((10.8, iter_step))
                w *= 0.5

        fin_th = th_bufs[cfg.NIT % 2]
        ai = 0
        tACT = 0.0
        gs_seq = [0]
        for (peb_tgt, op_, th_, tiles, t_ready) in evac_pts:
            while ai < len(a_steps) and tACT + a_steps[ai][0] <= t_ready:
                tACT += a_steps[ai][0]
                a_steps[ai][1]()
                ai += 1
            SC.wait(S.peb, peb_tgt)
            ocols = slice(op_ * W2, (op_ + 1) * W2)
            for n_, ti in enumerate(tiles):
                sl = gs_seq[0] % 2
                if gs_seq[0] - 2 >= 0:
                    SC.wait(gsp2[sl])
                SC.run("activation", out=gst[:, sl * W2:(sl + 1) * W2],
                       in_=psum[n_][:, 0:W2], func=AF.Silu, sem=S.evg)
                SC.wait(S.evg, S.evg.v)
                SC.dma(gsp2[sl], g_d[ti * 128:(ti + 1) * 128, ocols],
                       gst[:, sl * W2:(sl + 1) * W2])
                gs_seq[0] += 1
            tACT = max(tACT, t_ready) + len(tiles) * 1.5
        while ai < len(a_steps):
            a_steps[ai][1]()
            ai += 1
        chain_done = (S.chA.v, S.chD.v)
    flush_block(nc, "p2p3a", qs)

    xnT_ctx.__exit__(None, None, None)

    qs = new_queues()
    NGRP = len(cfg.groups3)
    NHP = len(cfg.hpanels)
    HPW = cfg.HP
    with (nc.sbuf_tensor("hmT", [128, 2 * cfg.G3 * T], BF16) as hmT,
          nc.sbuf_tensor("wdr", [128, 2 * cfg.G3 * HPW], BF16) as wdr,
          nc.sbuf_tensor("outs", [128, TT * HPW], F32) as outs,
          nc.sbuf_tensor("xg", [128, 4 * HPW], F32) as xg,
          contextlib.ExitStack() as pst):
        psum = [pst.enter_context(nc.psum_tensor(f"p3_{i}", [128, 1024], F32))
                for i in range(4)]
        SY, SC, VE, GP, TE = (qs[n] for n in ("sync", "scalar", "vector", "gpsimd", "tensor"))
        SCQ = qs["scalar"]
        for q in (SY, SCQ, GP, VE):
            q.wait(S.sp1)
            q.wait(gsp2[0])
            q.wait(gsp2[1])
            q.wait(sld2["A"])
            q.wait(sld2["D"])
            q.wait(asp2["A"])
            q.wait(asp2["D"])
            q.wait(S.hmsp)
            q.wait(hmld2[0])
            q.wait(hmld2[1])

        fl_hist = {}
        mm3_hist = {}
        xg_hist = {}
        for hpi, (hoff, hw) in enumerate(cfg.hpanels):
            for gi, (ic0, gn) in enumerate(cfg.groups3):
                gseq = hpi * NGRP + gi
                par = gseq % 2
                prev = divmod(gseq - 2, NGRP) if gseq - 2 >= 0 else None
                if prev is not None and (prev[0], prev[1], TT - 1) in fl_hist:
                    pv = fl_hist[(prev[0], prev[1], TT - 1)]
                    SCQ.wait(S.fl, pv)
                    SY.wait(S.fl, pv)
                    GP.wait(S.fl, pv)
                for c in range(gn):
                    ic = ic0 + c
                    slot = hmT[:, (par * cfg.G3 + c) * T:
                               (par * cfg.G3 + c) * T + T]
                    SCQ.dma(trh2[par], slot, hm_d[:, ic * 128:(ic + 1) * 128],
                            transpose=True)
                    GP.dma(wdld2[par], wdr[:, (par * cfg.G3 + c) * HPW:
                                           (par * cfg.G3 + c) * HPW + hw],
                           wdt_d[ic, :, hoff:hoff + hw])
                TE.wait(trh2[par])
                TE.wait(wdld2[par])
                for ti in range(TT):
                    pp = psum[ti % 4]
                    key = (hpi, gi, ti)
                    pk = _prev_psum_use(key, NGRP, TT)
                    if pk is not None and pk in fl_hist:
                        TE.wait(S.fl, fl_hist[pk])
                    nsub = (hw + 511) // 512
                    for c in range(gn):
                        lhs = hmT[:, (par * cfg.G3 + c) * T + ti * 128:
                                  (par * cfg.G3 + c) * T + (ti + 1) * 128]
                        for si, sub in enumerate(range(0, hw, 512)):
                            sw = min(512, hw - sub)
                            last = (c == gn - 1 and si == nsub - 1)
                            TE.run("matmul", pp[:, sub:sub + sw], lhs,
                                   wdr[:, (par * cfg.G3 + c) * HPW + sub:
                                       (par * cfg.G3 + c) * HPW + sub + sw],
                                   start=(c == 0), stop=(c == gn - 1),
                                   sem=(S.mm1 if last else None))
                    mm3_hist[(hpi, gi, ti)] = S.mm1.v
                    VE.wait(S.mm1, mm3_hist[(hpi, gi, ti)])
                    od = outs[:, ti * HPW: ti * HPW + hw]
                    if gi == 0:
                        if hpi > 0:
                            VE.wait(S.fl, fl_hist[(hpi - 1, NGRP - 1, ti)])
                            VE.wait(S.outp)
                        VE.run("tensor_copy", od, pp[:, 0:hw], sem=S.fl)
                    else:
                        VE.wait(S.fl, fl_hist[(hpi, gi - 1, ti)])
                        VE.run("tensor_tensor", out=od, in0=od, in1=pp[:, 0:hw],
                               op=aop.add, sem=S.fl)
                    fl_hist[(hpi, gi, ti)] = S.fl.v
                    if gi == NGRP - 1:
                        sl = ti % 2
                        xs = xg[:, sl * HPW: sl * HPW + hw]
                        gs_ = xg[:, (2 + sl) * HPW: (2 + sl) * HPW + hw]
                        if (hpi, ti - 2) in xg_hist:
                            SY.wait(S.fl, xg_hist[(hpi, ti - 2)])
                        elif hpi > 0 and (hpi - 1, ti + TT - 2) in xg_hist:
                            SY.wait(S.fl, xg_hist[(hpi - 1, ti + TT - 2)])
                        SY.dma(xgld2[sl], xs,
                               x_d[ti * 128:(ti + 1) * 128, hoff:hoff + hw])
                        SY.dma(xgld2[sl], gs_,
                               g_d[ti * 128:(ti + 1) * 128, hoff:hoff + hw])
                        VE.wait(xgld2[sl])
                        VE.wait(S.fl)
                        VE.run("tensor_tensor", out=od, in0=od, in1=gs_,
                               op=aop.mult, sem=S.fl)
                        VE.wait(S.fl)
                        VE.run("tensor_tensor", out=od, in0=od, in1=xs,
                               op=aop.add, sem=S.fl)
                        xg_hist[(hpi, ti)] = S.fl.v
                        fl_hist[(hpi, gi, ti)] = S.fl.v
                        SY.wait(S.fl, S.fl.v)
                        SY.dma(S.outp, out_d[ti * 128:(ti + 1) * 128, hoff:hoff + hw],
                               od)
        SY.wait(S.outp, S.outp.v)
        qs["gpsimd"].wait(S.outp, S.outp.v)
    flush_block(nc, "p3b", qs)

    st.close()
    return nc


def _prev_psum_use(key, ngrp, tt):
    hpi, gi, ti = key
    ti2 = ti - 4
    if ti2 >= 0:
        return (hpi, gi, ti2)
    gi2 = gi - 1
    if gi2 >= 0:
        return (hpi, gi2, tt + ti2)
    if hpi > 0:
        return (hpi - 1, ngrp - 1, tt + ti2)
    return None


def _host_pack(cfg, w_gate, w_up, w_down, packed, scales):
    H, I, O, KC, BLK, IC = cfg.H, cfg.I, cfg.O, cfg.KC, cfg.BLK, cfg.IC
    blockperm = np.array([2 * (p % 64) + (p // 64) for p in range(128)])
    sig = (np.arange(H) // 128) * 128 + blockperm[np.arange(H) % 128]

    def pack_w(w):
        wt = np.ascontiguousarray(w.T[sig])
        pieces = []
        for (poff, W) in cfg.panels1:
            blk = wt[:, poff:poff + W].reshape(KC, 128, W)
            for g in range(cfg.NGR):
                ck = min(cfg.GK, KC - g * cfg.GK)
                pieces.append(np.ascontiguousarray(blk[g * cfg.GK:g * cfg.GK + ck]).ravel())
        return np.concatenate(pieces)

    wg_flat = pack_w(np.asarray(w_gate, np.float32))
    wu_flat = pack_w(np.asarray(w_up, np.float32))
    wdt = np.ascontiguousarray(np.asarray(w_down, np.float32).T).reshape(IC, 128, O)
    pkt = np.ascontiguousarray(np.asarray(packed, np.int32).transpose(1, 2, 0))
    sc = np.asarray(scales, np.float32)[:, :, 0].T
    scb = np.ascontiguousarray(np.broadcast_to(sc[:, None, :], (BLK, 128, O)))
    return wg_flat, wu_flat, wdt, pkt, scb


def kernel(L0_attn_x, L0_attn_w_gate, L0_attn_w_up, L0_attn_w_down,
           L0_ffn_packed_weight, L0_ffn_scales):
    cfg = Cfg()
    x = np.asarray(L0_attn_x, np.float32)
    B = x.shape[0]
    wg_flat, wu_flat, wdt, pkt, scb = _host_pack(
        cfg, L0_attn_w_gate, L0_attn_w_up, L0_attn_w_down,
        L0_ffn_packed_weight, L0_ffn_scales)
    nc = build_nc(cfg)
    in_maps = []
    for c in range(B):
        in_maps.append(dict(x=np.ascontiguousarray(x[c]), wg=wg_flat, wu=wu_flat,
                            wdt=wdt, pkt=pkt, scb=scb))
    res = run_bass_kernel_spmd(nc, in_maps, core_ids=list(range(B)))
    out = np.stack([np.asarray(r["out"], np.float32) for r in res.results])
    return out


# version 13 (final)
# speedup vs baseline: 1.2528x; 1.0520x over previous
import contextlib
import numpy as np
import ml_dtypes
import concourse.bass as bass
import concourse.mybir as mybir
from concourse.bass_utils import run_bass_kernel_spmd
from concourse.alu_op_type import AluOpType as aop

F32 = mybir.dt.float32
F16 = mybir.dt.float16
BF16 = mybir.dt.bfloat16
I32 = mybir.dt.int32
F8 = mybir.dt.float8e4
AF = mybir.ActivationFunctionType
AX = mybir.AxisListType
bfnp = ml_dtypes.bfloat16


class Cfg:
    def __init__(self, T=2048, H=4096, I=11008, O=4096, NIT=14, n_cores=8):
        self.T, self.H, self.I, self.O = T, H, I, O
        self.NIT = NIT
        self.n_cores = n_cores
        self.TT = T // 128
        self.KC = H // 128
        self.IC = I // 128
        self.BLK = H // 128
        self.K = I // 2
        self.W1 = 384
        ws = []
        off = 0
        while off < I:
            w = min(self.W1, I - off)
            ws.append((off, w))
            off += w
        self.panels1 = ws
        self.GK = 4
        self.NGR = (self.KC + self.GK - 1) // self.GK
        self.TG = 2
        assert self.TT % self.TG == 0
        assert self.TT <= 16
        self.W2 = 512 if O >= 512 else O
        assert O % self.W2 == 0
        self.OP = O // self.W2
        self.TH2 = (self.TT + 7) // 8
        self.BB = 2
        assert self.BLK % self.BB == 0
        self.G3 = 8
        gs = []
        ic = 0
        while ic < self.IC:
            g = min(self.G3, self.IC - ic)
            gs.append((ic, g))
            ic += g
        self.groups3 = gs
        self.HP = 1024 if O >= 1024 else O
        hp = []
        off = 0
        while off < O:
            w = min(self.HP, O - off)
            hp.append((off, w))
            off += w
        self.hpanels = hp
        self.TH0 = 8.0
        self.W0 = 4.0


class Sem:
    def __init__(self, st, nc, name):
        self.h = st.enter_context(nc.semaphore(name))
        self.v = 0


class Q:

    def __init__(self, name):
        self.name = name
        self.ops = []
        self._lastwait = {}

    def add(self, f):
        self.ops.append(f)

    def wait(self, sem, val=None):
        v = sem.v if val is None else val
        if v <= 0:
            return
        if self._lastwait.get(id(sem), -1) >= v:
            return
        self._lastwait[id(sem)] = v
        self.ops.append(lambda e, h=sem.h, v=v: e.wait_ge(h, v))

    @staticmethod
    def _ap(x):
        if isinstance(x, bass.AP) or not hasattr(x, "ap"):
            return x
        return x.ap()

    def dma(self, sem, out, in_, **kw):
        sem.v += 16
        out, in_ = self._ap(out), self._ap(in_)
        def f(e, out=out, in_=in_, kw=kw, h=sem.h):
            e.dma_start(out=out, in_=in_, **kw).then_inc(h, 16)
        self.ops.append(f)

    def run(self, fname, *args, sem=None, inc=1, **kw):
        if sem is not None:
            sem.v += inc
        args = tuple(self._ap(a) for a in args)
        kw = {k: (self._ap(v) if k in ("out", "in_", "in0", "in1") else v)
              for k, v in kw.items()}
        def f(e, fname=fname, args=args, kw=kw, sem=sem, inc=inc):
            r = getattr(e, fname)(*args, **kw)
            if sem is not None:
                r.then_inc(sem.h, inc)
        self.ops.append(f)


def flush_block(nc, name, qs):
    with nc.Block(name) as blk:
        for q in qs.values():
            if q.ops:
                runner = getattr(blk, q.name)
                def body(eng, ops=q.ops):
                    for f in ops:
                        f(eng)
                runner(body)


def new_queues():
    return {n: Q(n) for n in ("sync", "scalar", "vector", "gpsimd", "tensor")}


def build_nc(cfg: Cfg):
    T, H, I, O = cfg.T, cfg.H, cfg.I, cfg.O
    TT, KC, IC, BLK = cfg.TT, cfg.KC, cfg.IC, cfg.BLK
    nc = bass.Bass("TRN2", target_bir_lowering=False)
    st = contextlib.ExitStack()

    x_d = nc.dram_tensor("x", [T, H], F32, kind="ExternalInput").ap()
    ng = 4 * 128
    wg_sz = H * I
    wg_d = nc.dram_tensor("wg", [wg_sz], F32, kind="ExternalInput").ap()
    wu_d = nc.dram_tensor("wu", [wg_sz], F32, kind="ExternalInput").ap()
    wdt_d = nc.dram_tensor("wdt", [IC, 128, O], F32, kind="ExternalInput").ap()
    pkt_d = nc.dram_tensor("pkt", [BLK, 64, O], I32, kind="ExternalInput").ap()
    scb_d = nc.dram_tensor("scb", [BLK, 128, O], F32, kind="ExternalInput").ap()
    out_d = nc.dram_tensor("out", [T, O], F32, kind="ExternalOutput").ap()
    xn_d = nc.dram_tensor("xn_s", [T, H], BF16, kind="Internal").ap()
    s_d = nc.dram_tensor("s_s", [TT, 128, I], F16, kind="Internal").ap()
    sup_d = nc.dram_tensor("sup_s", [TT, 128, I], BF16, kind="Internal").ap()
    mask_d = nc.dram_tensor("mask_s", [TT, 128, I], F8, kind="Internal").ap()
    hm_d = nc.dram_tensor("hm_s", [T, I], BF16, kind="Internal").ap()
    g_d = nc.dram_tensor("g_s", [T, O], F32, kind="Internal").ap()

    gran_off = {}
    off = 0
    for p, (poff, W) in enumerate(cfg.panels1):
        for g in range(cfg.NGR):
            ck = min(cfg.GK, KC - g * cfg.GK)
            gran_off[(p, g)] = (off, ck, W)
            off += ck * 128 * W
    assert off == wg_sz

    sems = {}
    for name in ("xld", "v0", "a0", "spxn", "trx",
                 "pegr", "mm1", "e1a", "e1v", "sp1",
                 "pkld", "scld", "dq", "evg", "gsp", "peb",
                 "sld", "chA", "chD", "mk", "msp",
                 "hmld", "hmtt", "hmsp", "trh", "wdld", "xgld", "fl", "outp"):
        sems[name] = Sem(st, nc, name)
    S = type("S", (), sems)
    wld = [Sem(st, nc, f"wld{g}") for g in range(cfg.NGR)]
    xld2 = [Sem(st, nc, f"xld{i}") for i in range(2)]
    pkld2 = [Sem(st, nc, f"pkld{i}") for i in range(2)]
    scld2 = [Sem(st, nc, f"scld{i}") for i in range(2)]
    sld2 = {c: Sem(st, nc, f"sld{c}") for c in "AD"}
    msp2 = {c: Sem(st, nc, f"msp{c}") for c in "AD"}
    hmld2 = [Sem(st, nc, f"hmld{i}") for i in range(2)]
    trh2 = [Sem(st, nc, f"trh{i}") for i in range(2)]
    wdld2 = [Sem(st, nc, f"wdld{i}") for i in range(2)]
    xgld2 = [Sem(st, nc, f"xgld{i}") for i in range(2)]

    stats = st.enter_context(nc.sbuf_tensor("stats", [128, 16], F32))
    th_a = st.enter_context(nc.sbuf_tensor("th_a", [128, TT], F32))
    th_b = st.enter_context(nc.sbuf_tensor("th_b", [128, TT], F32))
    cnt_t = st.enter_context(nc.sbuf_tensor("cnt_t", [128, TT], F32))
    sgn_t = st.enter_context(nc.sbuf_tensor("sgn_t", [128, TT], F32))
    xnT_ctx = nc.sbuf_tensor("xnT", [128, KC * T], BF16)
    xnT = xnT_ctx.__enter__()

    qs = new_queues()
    with (nc.sbuf_tensor("xbuf", [128, 2 * H], F32) as xbuf,
          nc.sbuf_tensor("xnst", [128, 2 * H], BF16) as xnst,
          nc.sbuf_tensor("junk0", [128, H], F8) as junk8):
        SY, SC, VE = qs["sync"], qs["scalar"], qs["vector"]
        rel_x = {}
        for tt in range(TT):
            par = tt % 2
            xb = xbuf[:, par * H:(par + 1) * H]
            xs = xnst[:, par * H:(par + 1) * H]
            if tt - 2 >= 0:
                for sem, val in rel_x[tt - 2]:
                    SY.wait(sem, val)
            SY.dma(xld2[par], xb, x_d[tt * 128:(tt + 1) * 128, :])
            c = (tt % 2) * 8
            VE.wait(xld2[par])
            VE.run("reduce_sum", stats[:, c:c + 1], xb, axis=AX.X, sem=S.v0)
            VE.wait(S.v0)
            VE.run("tensor_scalar", out=stats[:, c + 1:c + 2], in0=stats[:, c:c + 1],
                   scalar1=-1.0 / H, scalar2=None, op0=aop.mult, sem=S.v0)
            SC.wait(S.v0)
            SC.wait(xld2[par], xld2[par].v)
            SC.run("activation", out=junk8[:, :H], in_=xb, func=AF.Square,
                   bias=stats[:, c + 1:c + 2], scale=1.0,
                   accum_out=stats[:, c + 2:c + 3], sem=S.a0)
            VE.wait(S.a0)
            VE.run("tensor_scalar", out=stats[:, c + 3:c + 4], in0=stats[:, c + 2:c + 3],
                   scalar1=1.0 / H, scalar2=1e-5, op0=aop.mult, op1=aop.add, sem=S.v0)
            SC.wait(S.v0)
            SC.run("activation", out=stats[:, c + 4:c + 5], in_=stats[:, c + 3:c + 4],
                   func=AF.Sqrt, sem=S.a0)
            VE.wait(S.a0)
            VE.run("reciprocal", stats[:, c + 5:c + 6], stats[:, c + 4:c + 5], sem=S.v0)
            VE.wait(S.v0)
            if tt - 2 >= 0:
                VE.wait(S.spxn)
            xs_perm = xs.rearrange("p (b r c) -> p b c r", b=KC, r=2)
            VE.run("tensor_scalar", out=xs_perm, in0=xb,
                   scalar1=stats[:, c + 1:c + 2], scalar2=stats[:, c + 5:c + 6],
                   op0=aop.add, op1=aop.mult, sem=S.v0)
            rel_x[tt] = [(S.a0, S.a0.v), (S.v0, S.v0.v)]
            SY.wait(S.v0)
            SY.dma(S.spxn, xn_d[tt * 128:(tt + 1) * 128, :], xs)
        SCQ = qs["scalar"]
        SCQ.wait(S.spxn)
        for k in range(KC):
            SCQ.dma(S.trx, xnT[:, k * T:(k + 1) * T],
                    xn_d[:, k * 128:(k + 1) * 128], transpose=True)
    flush_block(nc, "p0", qs)

    qs = new_queues()
    slotW = cfg.GK * cfg.W1
    with (nc.sbuf_tensor("wgr", [128, cfg.NGR * slotW], BF16) as wgr,
          nc.sbuf_tensor("wur", [128, cfg.NGR * slotW], BF16) as wur,
          nc.sbuf_tensor("sst", [128, 2 * cfg.TG * cfg.W1], F16) as sst,
          nc.sbuf_tensor("supst", [128, 2 * cfg.TG * cfg.W1], BF16) as supst,
          contextlib.ExitStack() as pst):
        psum = [pst.enter_context(nc.psum_tensor(f"p1_{i}", [128, 512], F32))
                for i in range(8)]
        SY, SC, VE, GP, TE = (qs[n] for n in ("sync", "scalar", "vector", "gpsimd", "tensor"))
        TE.wait(S.trx)
        NGROUP = TT // cfg.TG
        npan = len(cfg.panels1)
        ev_hist = {}
        sp_hist = {}
        gq_hist = {}
        def load_gran(p, g):
            offs, ck, W = gran_off[(p, g)]
            if p > 0:
                if g < cfg.NGR - 1:
                    GP.wait(S.pegr, (p - 1) * (cfg.NGR - 1) + g + 1)
                else:
                    GP.wait(S.mm1, p * NGROUP)
            for (ring, src) in ((wgr, wg_d), (wur, wu_d)):
                dst = ring[:, g * slotW + 0: g * slotW + ck * W]
                dst = dst.rearrange("p (c w) -> p c w", c=ck)
                srcap = src[offs: offs + ck * 128 * W]
                srcap = srcap.rearrange("(c p w) -> p c w", c=ck, p=128)
                GP.dma(wld[g], dst, srcap)
        for g in range(cfg.NGR):
            load_gran(0, g)
        gidx = 0
        for p, (poff, W) in enumerate(cfg.panels1):
            for tg in range(NGROUP):
                par = tg % 2
                tiles = (cfg.TG * tg, cfg.TG * tg + 1)
                if gidx - 2 >= 0:
                    ea, ev = ev_hist[gidx - 2]
                    TE.wait(S.e1a, ea)
                    TE.wait(S.e1v, ev)
                for k in range(KC):
                    g = k // cfg.GK
                    if tg == 0:
                        TE.wait(wld[g], 32 * (p + 1))
                    ck_local = k % cfg.GK
                    woff = g * slotW + ck_local * W
                    for tl, ti in enumerate(tiles):
                        lhs = xnT[:, k * T + ti * 128: k * T + (ti + 1) * 128]
                        pg = psum[par * 4 + 2 * tl][:, 0:W]
                        pu = psum[par * 4 + 2 * tl + 1][:, 0:W]
                        kw = dict(start=(k == 0), stop=(k == KC - 1))
                        last_of_group = (k == KC - 1 and tl == len(tiles) - 1)
                        rel_gran = (tg == NGROUP - 1 and ck_local == 0 and g > 0
                                    and tl == 0)
                        if rel_gran:
                            TE.run("matmul", pg, lhs, wgr[:, woff:woff + W],
                                   sem=S.pegr, **kw)
                            if p + 1 < npan:
                                load_gran(p + 1, g - 1)
                        else:
                            TE.run("matmul", pg, lhs, wgr[:, woff:woff + W], **kw)
                        if last_of_group:
                            TE.run("matmul", pu, lhs, wur[:, woff:woff + W],
                                   sem=S.mm1, **kw)
                        else:
                            TE.run("matmul", pu, lhs, wur[:, woff:woff + W], **kw)
                gq_hist[gidx] = S.mm1.v
                if tg == NGROUP - 1 and p + 1 < npan:
                    load_gran(p + 1, cfg.NGR - 1)
                SC.wait(S.mm1, gq_hist[gidx])
                if gidx - 2 >= 0:
                    SC.wait(S.sp1)
                for tl, ti in enumerate(tiles):
                    sslice = sst[:, (par * cfg.TG + tl) * cfg.W1:
                                 (par * cfg.TG + tl) * cfg.W1 + W]
                    SC.run("activation", out=sslice, in_=psum[par * 4 + 2 * tl][:, 0:W],
                           func=AF.Silu, sem=S.e1a)
                ev_a = S.e1a.v
                VE.wait(S.e1a, ev_a)
                if gidx - 2 >= 0:
                    VE.wait(S.sp1)
                for tl, ti in enumerate(tiles):
                    sslice = sst[:, (par * cfg.TG + tl) * cfg.W1:
                                 (par * cfg.TG + tl) * cfg.W1 + W]
                    supslice = supst[:, (par * cfg.TG + tl) * cfg.W1:
                                     (par * cfg.TG + tl) * cfg.W1 + W]
                    VE.run("tensor_tensor", out=supslice, in0=sslice,
                           in1=psum[par * 4 + 2 * tl + 1][:, 0:W], op=aop.mult,
                           sem=S.e1v)
                ev_v = S.e1v.v
                ev_hist[gidx] = (ev_a, ev_v)
                SY.wait(S.e1a, ev_a)
                SY.wait(S.e1v, ev_v)
                for tl, ti in enumerate(tiles):
                    sslice = sst[:, (par * cfg.TG + tl) * cfg.W1:
                                 (par * cfg.TG + tl) * cfg.W1 + W]
                    supslice = supst[:, (par * cfg.TG + tl) * cfg.W1:
                                     (par * cfg.TG + tl) * cfg.W1 + W]
                    SY.dma(S.sp1, s_d[ti, :, poff:poff + W], sslice)
                    SY.dma(S.sp1, sup_d[ti, :, poff:poff + W], supslice)
                sp_hist[gidx] = S.sp1.v
                gidx += 1
    flush_block(nc, "p1", qs)

    qs = new_queues()
    W2, BB = cfg.W2, cfg.BB
    bw = BB * W2
    NB = BLK // BB
    with (nc.sbuf_tensor("pk2", [128, 2 * bw], I32) as pk2,
          nc.sbuf_tensor("nib", [128, 2 * bw], I32) as nib,
          nc.sbuf_tensor("scb2", [128, 2 * bw], F32) as scb2,
          nc.sbuf_tensor("wqf", [128, BLK * W2], BF16) as wqf,
          nc.sbuf_tensor("gst", [128, 8 * W2], F32) as gst,
          contextlib.ExitStack() as pst):
        psum = [pst.enter_context(nc.psum_tensor(f"p2_{i}", [128, 512], F32))
                for i in range(8)]
        SY, SC, VE, GP, TE = (qs[n] for n in ("sync", "scalar", "vector", "gpsimd", "tensor"))
        TE.wait(S.trx)
        for q in (SY, SC, VE, GP):
            q.wait(S.sp1)
        peb_hist = {}
        dq_hist = {}
        dqb_hist = {}
        evg_hist = {}
        for op_ in range(cfg.OP):
            ocols = slice(op_ * W2, (op_ + 1) * W2)
            for bb in range(NB):
                par = bb % 2
                b0 = bb * BB
                bidx = op_ * NB + bb
                if bidx - 2 >= 0:
                    GP.wait(S.dq, dqb_hist[bidx - 2])
                    SY.wait(S.dq, dqb_hist[bidx - 2])
                src = pkt_d[b0:b0 + BB, :, ocols].rearrange("b p w -> p b w")
                dst0 = pk2[0:64, par * bw:(par + 1) * bw].rearrange(
                    "p (b w) -> p b w", b=BB)
                dst1 = pk2[64:128, par * bw:(par + 1) * bw].rearrange(
                    "p (b w) -> p b w", b=BB)
                GP.dma(pkld2[par], dst0, src)
                GP.dma(pkld2[par], dst1, src)
                scsrc = scb_d[b0:b0 + BB, :, ocols].rearrange("b p w -> p b w")
                scdst = scb2[:, par * bw:(par + 1) * bw].rearrange(
                    "p (b w) -> p b w", b=BB)
                SY.dma(scld2[par], scdst, scsrc)
                VE.wait(pkld2[par])
                if op_ > 0:
                    VE.wait(S.peb, peb_hist[(op_ - 1, b0 + BB - 1)])
                VE.run("tensor_scalar", out=nib[0:64, par * bw:(par + 1) * bw],
                       in0=pk2[0:64, par * bw:(par + 1) * bw], scalar1=4,
                       scalar2=None, op0=aop.logical_shift_right, sem=S.dq)
                VE.run("tensor_scalar", out=nib[64:128, par * bw:(par + 1) * bw],
                       in0=pk2[64:128, par * bw:(par + 1) * bw], scalar1=15,
                       scalar2=None, op0=aop.bitwise_and, sem=S.dq)
                VE.wait(scld2[par])
                VE.wait(S.dq)
                VE.run("scalar_tensor_tensor", out=wqf[:, b0 * W2:(b0 + BB) * W2],
                       in0=nib[:, par * bw:(par + 1) * bw], scalar=8.0,
                       in1=scb2[:, par * bw:(par + 1) * bw], op0=aop.subtract,
                       op1=aop.mult, sem=S.dq)
                for bl in range(BB):
                    dq_hist[(op_, b0 + bl)] = S.dq.v
                dqb_hist[bidx] = S.dq.v
            for th_ in range(cfg.TH2):
                tiles = list(range(th_ * 8, min(TT, th_ * 8 + 8)))
                for b in range(BLK):
                    TE.wait(S.dq, dq_hist[(op_, b)])
                    if op_ > 0 and b == 0 and th_ == 0:
                        TE.wait(S.evg, evg_hist[op_ - 1])
                    if th_ == 1 and b == 0:
                        TE.wait(S.evg, S.evg.v)
                    rhs = wqf[:, b * W2:(b + 1) * W2]
                    for n_, ti in enumerate(tiles):
                        lhs = xnT[:, b * T + ti * 128: b * T + (ti + 1) * 128]
                        pq = psum[n_][:, 0:W2]
                        last = (b == BLK - 1 and n_ == len(tiles) - 1)
                        TE.run("matmul", pq, lhs, rhs,
                               start=(b == 0), stop=(b == BLK - 1),
                               sem=(S.peb if last or (th_ == cfg.TH2 - 1 and n_ == len(tiles) - 1) else None))
                    if th_ == cfg.TH2 - 1:
                        peb_hist[(op_, b)] = S.peb.v
                SC.wait(S.peb, S.peb.v)
                SC.wait(S.gsp)
                for n_, ti in enumerate(tiles):
                    SC.run("activation", out=gst[:, n_ * W2:(n_ + 1) * W2],
                           in_=psum[n_][:, 0:W2], func=AF.Silu, sem=S.evg)
                SY.wait(S.evg, S.evg.v)
                for n_, ti in enumerate(tiles):
                    SY.dma(S.gsp, g_d[ti * 128:(ti + 1) * 128, ocols],
                           gst[:, n_ * W2:(n_ + 1) * W2])
            evg_hist[op_] = S.evg.v
    flush_block(nc, "p2", qs)

    qs = new_queues()
    QH = (I + 3) // 4
    with (nc.sbuf_tensor("abufA", [128, I], F16) as abufA,
          nc.sbuf_tensor("abufD", [128, I], F16) as abufD,
          nc.sbuf_tensor("junkA", [128, I], F8) as junkA,
          nc.sbuf_tensor("junkD", [128, I], F8) as junkD,
          nc.sbuf_tensor("supq", [128, 2 * QH], BF16) as supq):
        SY, SC, VE = qs["sync"], qs["scalar"], qs["vector"]
        tiles_A = [t for t in range(TT) if t % 2 == 0]
        tiles_D = [t for t in range(TT) if t % 2 == 1]
        Kh = cfg.K + 0.5

        VE.run("memset", th_a[:, :], cfg.TH0, sem=S.chD)
        th_bufs = [th_a, th_b]
        mk_hist = {}
        hq_seq = [0]
        hq_tt = {}
        hq_sp = {}

        def load_tile(cid, n, j, abuf):
            if n - 1 >= 0:
                SY.wait(S.mk, mk_hist[(cid, n - 1)])
            SY.dma(sld2[cid], abuf, s_d[j, :, :])

        def bisect_A(j, abuf, jbuf):
            SC.wait(sld2["A"])
            SC.wait(S.chD, 1)
            SC.run("activation", out=abuf, in_=abuf, func=AF.Abs, sem=S.chA)
            w = cfg.W0
            for it in range(cfg.NIT):
                cur = th_bufs[it % 2]
                nxt = th_bufs[(it + 1) % 2]
                SC.wait(S.chA)
                SC.run("activation", out=jbuf, in_=abuf, func=AF.Sign,
                       bias=cur[:, j:j + 1], scale=-1.0,
                       accum_out=cnt_t[:, j:j + 1], sem=S.chA)
                SC.wait(S.chA)
                SC.run("activation", out=sgn_t[:, j:j + 1],
                       in_=cnt_t[:, j:j + 1], func=AF.Sign, scale=-1.0,
                       bias=0.0, sem=S.chA)
                SC.wait(S.chA)
                SC.run("activation", out=nxt[:, j:j + 1],
                       in_=sgn_t[:, j:j + 1], func=AF.Identity, scale=float(w),
                       bias=cur[:, j:j + 1], sem=S.chA)
                w *= 0.5

        def bisect_D(j, abuf, jbuf):
            VE.wait(sld2["D"])
            VE.run("scalar_tensor_tensor", out=abuf, in0=abuf, scalar=-1.0,
                   in1=abuf, op0=aop.mult, op1=aop.max, sem=S.chD)
            w = cfg.W0
            for it in range(cfg.NIT):
                cur = th_bufs[it % 2]
                nxt = th_bufs[(it + 1) % 2]
                VE.wait(S.chD)
                VE.run("tensor_scalar", out=jbuf, in0=abuf,
                       scalar1=cur[:, j:j + 1], scalar2=None, op0=aop.is_gt,
                       op1=aop.add, accum_out=cnt_t[:, j:j + 1], sem=S.chD)
                VE.wait(S.chD)
                VE.run("tensor_scalar", out=sgn_t[:, j:j + 1],
                       in0=cnt_t[:, j:j + 1], scalar1=float(Kh), scalar2=None,
                       op0=aop.is_gt, sem=S.chD)
                VE.wait(S.chD)
                VE.run("tensor_scalar", out=sgn_t[:, j:j + 1],
                       in0=sgn_t[:, j:j + 1], scalar1=float(2 * w),
                       scalar2=float(w), op0=aop.mult, op1=aop.subtract,
                       sem=S.chD)
                VE.wait(S.chD)
                VE.run("tensor_tensor", out=nxt[:, j:j + 1],
                       in0=cur[:, j:j + 1], in1=sgn_t[:, j:j + 1],
                       op=aop.add, sem=S.chD)
                w *= 0.5

        def hmid_tile(cid, n, j, abuf, ch):
            fin = th_bufs[cfg.NIT % 2]
            VE.wait(ch)
            for qtr in range(4):
                c0 = qtr * QH
                cw = min(QH, I - c0)
                seq = hq_seq[0]
                par = seq % 2
                sq = supq[:, par * QH: par * QH + cw]
                if seq - 2 >= 0:
                    SY.wait(S.mk, hq_tt[seq - 2])
                    SY.wait(S.hmsp)
                SY.dma(hmld2[par], sq, sup_d[j, :, c0:c0 + cw])
                VE.wait(hmld2[par])
                if seq - 2 >= 0:
                    VE.wait(S.hmsp)
                VE.run("scalar_tensor_tensor", out=sq, in0=abuf[:, c0:c0 + cw],
                       scalar=fin[:, j:j + 1], in1=sq, op0=aop.is_gt,
                       op1=aop.mult, sem=S.mk)
                hq_tt[seq] = S.mk.v
                SY.wait(S.mk, hq_tt[seq])
                SY.dma(S.hmsp, hm_d[j * 128:(j + 1) * 128, c0:c0 + cw], sq)
                hq_sp[seq] = S.hmsp.v
                hq_seq[0] += 1
            mk_hist[(cid, n)] = S.mk.v

        SY.wait(S.sp1)
        SY.wait(S.gsp)
        for n in range(len(tiles_A)):
            jA, jD = tiles_A[n], tiles_D[n]
            load_tile("A", n, jA, abufA)
            bisect_A(jA, abufA, junkA)
            load_tile("D", n, jD, abufD)
            bisect_D(jD, abufD, junkD)
            hmid_tile("D", n, jD, abufD, S.chD)
            hmid_tile("A", n, jA, abufA, S.chA)
    flush_block(nc, "p3a", qs)

    xnT_ctx.__exit__(None, None, None)

    qs = new_queues()
    NGRP = len(cfg.groups3)
    NHP = len(cfg.hpanels)
    HPW = cfg.HP
    with (nc.sbuf_tensor("hmT", [128, 2 * cfg.G3 * T], BF16) as hmT,
          nc.sbuf_tensor("wdr", [128, 2 * cfg.G3 * HPW], BF16) as wdr,
          nc.sbuf_tensor("outs", [128, TT * HPW], F32) as outs,
          nc.sbuf_tensor("xg", [128, 8 * HPW], F32) as xg,
          contextlib.ExitStack() as pst):
        psum = [pst.enter_context(nc.psum_tensor(f"p3_{i}", [128, 1024], F32))
                for i in range(4)]
        SY, SC, VE, GP, TE = (qs[n] for n in ("sync", "scalar", "vector", "gpsimd", "tensor"))
        for q in (SY, qs["scalar"], GP, VE):
            q.wait(S.hmsp)
            q.wait(S.sp1)
            q.wait(S.gsp)
            q.wait(sld2["A"])
            q.wait(sld2["D"])
            q.wait(hmld2[0])
            q.wait(hmld2[1])

        fl_hist = {}
        mm3_hist = {}
        xg_hist = {}

        def xg_load(hpi, ti, hoff, hw):
            sl = ti % 4
            xs = xg[:, sl * HPW: sl * HPW + hw]
            gs_ = xg[:, (4 + sl) * HPW: (4 + sl) * HPW + hw]
            if (hpi, ti - 4) in xg_hist:
                SY.wait(S.fl, xg_hist[(hpi, ti - 4)][1])
            elif hpi > 0 and (hpi - 1, ti + TT - 4) in xg_hist:
                SY.wait(S.fl, xg_hist[(hpi - 1, ti + TT - 4)][1])
            SY.dma(xgld2[ti % 2], xs, x_d[ti * 128:(ti + 1) * 128, hoff:hoff + hw])
            SY.dma(xgld2[ti % 2], gs_, g_d[ti * 128:(ti + 1) * 128, hoff:hoff + hw])
            return xs, gs_

        for hpi, (hoff, hw) in enumerate(cfg.hpanels):
            for gi, (ic0, gn) in enumerate(cfg.groups3):
                gseq = hpi * NGRP + gi
                par = gseq % 2
                step = (hpi, gi)
                prev = divmod(gseq - 2, NGRP) if gseq - 2 >= 0 else None
                if prev is not None and (prev[0], prev[1], TT - 1) in fl_hist:
                    pv = fl_hist[(prev[0], prev[1], TT - 1)]
                    qs["scalar"].wait(S.fl, pv)
                    SY.wait(S.fl, pv)
                    GP.wait(S.fl, pv)
                for c in range(gn):
                    tq = qs["scalar"]
                    tq.dma(trh2[par], hmT[:, (par * cfg.G3 + c) * T:
                                          (par * cfg.G3 + c) * T + T],
                           hm_d[:, (ic0 + c) * 128:(ic0 + c + 1) * 128],
                           transpose=True)
                    GP.dma(wdld2[par], wdr[:, (par * cfg.G3 + c) * HPW:
                                           (par * cfg.G3 + c) * HPW + hw],
                           wdt_d[ic0 + c, :, hoff:hoff + hw])
                TE.wait(trh2[par])
                TE.wait(wdld2[par])
                if gi == NGRP - 1:
                    xg0 = xg_load(hpi, 0, hoff, hw)
                for ti in range(TT):
                    pp = psum[ti % 4]
                    key = (hpi, gi, ti)
                    pk = _prev_psum_use(key, NGRP, TT)
                    if pk is not None and pk in fl_hist:
                        TE.wait(S.fl, fl_hist[pk])
                    nsub = (hw + 511) // 512
                    for c in range(gn):
                        lhs = hmT[:, (par * cfg.G3 + c) * T + ti * 128:
                                  (par * cfg.G3 + c) * T + (ti + 1) * 128]
                        for si, sub in enumerate(range(0, hw, 512)):
                            sw = min(512, hw - sub)
                            last = (c == gn - 1 and si == nsub - 1)
                            TE.run("matmul", pp[:, sub:sub + sw], lhs,
                                   wdr[:, (par * cfg.G3 + c) * HPW + sub:
                                       (par * cfg.G3 + c) * HPW + sub + sw],
                                   start=(c == 0), stop=(c == gn - 1),
                                   sem=(S.mm1 if last else None))
                    mm3_hist[(hpi, gi, ti)] = S.mm1.v
                    VE.wait(S.mm1, mm3_hist[(hpi, gi, ti)])
                    od = outs[:, ti * HPW: ti * HPW + hw]
                    if gi == 0:
                        if hpi > 0:
                            VE.wait(S.fl, fl_hist[(hpi - 1, NGRP - 1, ti)])
                            VE.wait(S.outp)
                        VE.run("tensor_copy", od, pp[:, 0:hw], sem=S.fl)
                    else:
                        VE.wait(S.fl, fl_hist[(hpi, gi - 1, ti)])
                        VE.run("tensor_tensor", out=od, in0=od, in1=pp[:, 0:hw],
                               op=aop.add, sem=S.fl)
                    fl_hist[(hpi, gi, ti)] = S.fl.v
                    if gi == NGRP - 1:
                        if ti == 0:
                            xs, gs_ = xg0
                        else:
                            xs, gs_ = xg_cur
                        if ti + 1 < TT:
                            xg_cur = xg_load(hpi, ti + 1, hoff, hw)
                        VE.wait(xgld2[ti % 2])
                        VE.wait(S.fl)
                        VE.run("tensor_tensor", out=od, in0=od, in1=gs_,
                               op=aop.mult, sem=S.fl)
                        VE.wait(S.fl)
                        VE.run("tensor_tensor", out=od, in0=od, in1=xs,
                               op=aop.add, sem=S.fl)
                        xg_hist[(hpi, ti)] = (S.xgld.v, S.fl.v)
                        fl_hist[(hpi, gi, ti)] = S.fl.v
                        SY.wait(S.fl, S.fl.v)
                        SY.dma(S.outp, out_d[ti * 128:(ti + 1) * 128, hoff:hoff + hw],
                               od)
        SY.wait(S.outp, S.outp.v)
        qs["gpsimd"].wait(S.outp, S.outp.v)
    flush_block(nc, "p3b", qs)

    st.close()
    return nc


def _prev_psum_use(key, ngrp, tt):
    hpi, gi, ti = key
    ti2 = ti - 4
    if ti2 >= 0:
        return (hpi, gi, ti2)
    gi2 = gi - 1
    if gi2 >= 0:
        return (hpi, gi2, tt + ti2)
    if hpi > 0:
        return (hpi - 1, ngrp - 1, tt + ti2)
    return None


def _host_pack(cfg, w_gate, w_up, w_down, packed, scales):
    H, I, O, KC, BLK, IC = cfg.H, cfg.I, cfg.O, cfg.KC, cfg.BLK, cfg.IC
    blockperm = np.array([2 * (p % 64) + (p // 64) for p in range(128)])
    sig = (np.arange(H) // 128) * 128 + blockperm[np.arange(H) % 128]

    def pack_w(w):
        wt = np.ascontiguousarray(w.T[sig])
        pieces = []
        for (poff, W) in cfg.panels1:
            blk = wt[:, poff:poff + W].reshape(KC, 128, W)
            for g in range(cfg.NGR):
                ck = min(cfg.GK, KC - g * cfg.GK)
                pieces.append(np.ascontiguousarray(blk[g * cfg.GK:g * cfg.GK + ck]).ravel())
        return np.concatenate(pieces)

    wg_flat = pack_w(np.asarray(w_gate, np.float32))
    wu_flat = pack_w(np.asarray(w_up, np.float32))
    wdt = np.ascontiguousarray(np.asarray(w_down, np.float32).T).reshape(IC, 128, O)
    pkt = np.ascontiguousarray(np.asarray(packed, np.int32).transpose(1, 2, 0))
    sc = np.asarray(scales, np.float32)[:, :, 0].T
    scb = np.ascontiguousarray(np.broadcast_to(sc[:, None, :], (BLK, 128, O)))
    return wg_flat, wu_flat, wdt, pkt, scb


def kernel(L0_attn_x, L0_attn_w_gate, L0_attn_w_up, L0_attn_w_down,
           L0_ffn_packed_weight, L0_ffn_scales):
    cfg = Cfg()
    x = np.asarray(L0_attn_x, np.float32)
    B = x.shape[0]
    wg_flat, wu_flat, wdt, pkt, scb = _host_pack(
        cfg, L0_attn_w_gate, L0_attn_w_up, L0_attn_w_down,
        L0_ffn_packed_weight, L0_ffn_scales)
    nc = build_nc(cfg)
    in_maps = []
    for c in range(B):
        in_maps.append(dict(x=np.ascontiguousarray(x[c]), wg=wg_flat, wu=wu_flat,
                            wdt=wdt, pkt=pkt, scb=scb))
    res = run_bass_kernel_spmd(nc, in_maps, core_ids=list(range(B)))
    out = np.stack([np.asarray(r["out"], np.float32) for r in res.results])
    return out
